# version 10
# baseline (speedup 1.0000x reference)
import math
from contextlib import ExitStack

import numpy as np
import ml_dtypes

import concourse.bass as bass
import concourse.mybir as mybir
from concourse.bass_utils import run_bass_kernel_spmd

F32 = mybir.dt.float32
BF16 = mybir.dt.bfloat16
AF = mybir.ActivationFunctionType
ALU = mybir.AluOpType
AX = mybir.AxisListType

NCORE = 8
HID = 2048
KCN = 16
H = 16
HKV = 8
NHEADS = H + HKV
DFF = 1024
DCN = 8
MCN = 16
NRT = 16
NSH = 2
NE = NSH + NRT
EG = 3
NEG = NE // EG
TWO_PI = 6.283185307179586
INV_2PI = 1.0 / TWO_PI
MAGIC = 12582912.0
MASKV = 2.0 / 128.0

SEM_NAMES = (["mT", "cV", "cS", "cP", "dCY", "dCG", "dsp", "dxt", "dgr", "dst",
              "dxb", "dou", "dou1", "dat", "dcc", "dst1", "dst2", "dst3",
              "dwi0", "dwi1"]
             + [f"dx{i}" for i in range(2)] + [f"dg{i}" for i in range(2)]
             + [f"dn{i}" for i in range(2)] + [f"dw{i}" for i in range(3)]
             + [f"du{i}" for i in range(2)] + [f"dd{i}" for i in range(6)])


def build_program(TC=2048, quad=4):
    assert TC % 128 == 0
    NG = TC // 128
    ST = min(512, TC)
    NST = TC // ST
    STG = ST // 16
    TB = min(1024, TC)
    NTB = TC // TB
    NCH = max(1, TB // 512)
    CH = TB // NCH
    GTB = TB // 128
    SGC = ST // 16
    HPB = 16 // quad
    NTC2 = TC // 512
    RPH = quad * TC // 16
    HIC = max(1, 512 // RPH)

    nc = bass.Bass("TRN2", target_bir_lowering=False)

    din = lambda n, s, d=F32: nc.dram_tensor(n, s, d, kind="ExternalInput").ap()
    x_d = din("x", [TC, HID])
    xs_d = din("xs", [TC, HID])
    wq_d = din("wq", [16, 128, 2048])
    wk_d = din("wk", [8, 128, 2048])
    wv_d = din("wv", [8, 128, 2048])
    wo_d = din("wo", [16, 128, 2048])
    wr_d = din("wr", [128, 256])
    wup_d = din("wup", [NE, DCN, 128, 2048])
    wdn_d = din("wdn", [NE, MCN, 128, 1024])
    rt_d = din("rt", [128, 64])
    bq_d = din("bq", [128, 16])
    bk_d = din("bk", [128, 8])
    bv_d = din("bv", [128, 8])
    bo_d = din("bo", [128, 16])
    cos_d = din("cosf", [128, TC])
    sin_d = din("sinf", [128, TC])
    mask_d = din("maskc", [128, 256], BF16)
    id_d = din("identc", [128, 128], BF16)
    pm_d = din("pmat", [128, 128], BF16)
    offr_d = din("offr", [128, 1])
    offs_d = din("offs", [128, 1])

    out_d = nc.dram_tensor("out", [TC, HID], F32, kind="ExternalOutput").ap()
    xn_dram = nc.dram_tensor("xn_scr", [128, KCN * TC], BF16, kind="Internal").ap()
    xns_dram = nc.dram_tensor("xns_scr", [128, KCN * TC], BF16,
                              kind="Internal").ap()
    s_dram = nc.dram_tensor("s_scr", [128, MCN * TC], BF16, kind="Internal").ap()
    satt_dram = nc.dram_tensor("satt_scr", [128, 16 * TC], BF16,
                               kind="Internal").ap()

    ctx = ExitStack()
    E = ctx.enter_context
    sbuf = lambda n, s, d: E(nc.sbuf_tensor(n, s, d))

    rt_sb = sbuf("rt_sb", [128, 64], BF16)
    wr_sb = sbuf("wr_sb", [128, 256], BF16)
    mask_sb = sbuf("mask_sb", [128, 256], BF16)
    id_sb = sbuf("id_sb", [128, 128], BF16)
    pm_sb = sbuf("pm_sb", [128, 128], BF16)
    bq_sb = sbuf("bq_sb", [128, 16], F32)
    bk_sb = sbuf("bk_sb", [128, 8], F32)
    bv_sb = sbuf("bv_sb", [128, 8], F32)
    bo_sb = sbuf("bo_sb", [128, 16], F32)
    offr_sb = sbuf("offr_sb", [128, 1], F32)
    offs_sb = sbuf("offs_sb", [128, 1], F32)
    ones128 = sbuf("ones128", [128, 128], BF16)
    eps_sb = sbuf("eps_sb", [128, 1], F32)
    eps12_sb = sbuf("eps12_sb", [128, 1], F32)

    ph1_ctx = ExitStack()
    E1 = ph1_ctx.enter_context
    sb1 = lambda n, s, d: E1(nc.sbuf_tensor(n, s, d))
    x_tm = [sb1(f"x_tm{i}", [128, HID], F32) for i in range(2)]
    xsq = sb1("xsq", [128, HID], F32)
    st_s1 = sb1("st_s1", [128, 1], F32)
    st_negm = sb1("st_negm", [128, 1], F32)
    st_ss = sb1("st_ss", [128, 1], F32)
    st_var = sb1("st_var", [128, 1], F32)
    st_m2 = sb1("st_m2", [128, 1], F32)
    st_std = sb1("st_std", [128, 1], F32)
    st_rstd = sb1("st_rstd", [128, 1], F32)
    xn_tm = sb1("xn_tm", [128, HID], BF16)
    xn_stage = [sb1(f"xn_stage{i}", [128, HID], BF16) for i in range(2)]

    pb = [E(nc.psum_tensor(f"pb{i}", [128, 512], F32)) for i in range(8)]
    pbh = [p.bitcast(BF16) for p in pb]

    sems, cnt = {}, {}
    for s in SEM_NAMES:
        sems[s] = E(nc.semaphore(s))
        cnt[s] = 0
    seen = {e: {s: 0 for s in sems} for e in "TVSGY"}

    def wt(eng, ekey, s, c):
        if c > seen[ekey][s]:
            eng.wait_ge(sems[s], c)
            seen[ekey][s] = c

    def inc(inst, s, k=1):
        inst.then_inc(sems[s], k)
        cnt[s] += k

    blk = E(nc.Block())

    @blk.sync
    def _(y):
        for dst, src in [(mask_sb, mask_d), (id_sb, id_d), (pm_sb, pm_d),
                         (bq_sb, bq_d), (bk_sb, bk_d), (bv_sb, bv_d),
                         (bo_sb, bo_d), (offr_sb, offr_d), (offs_sb, offs_d)]:
            inc(y.dma_start(out=dst[:], in_=src), "dCY", 16)

    @blk.gpsimd
    def _(g):
        for dst, src in [(rt_sb, rt_d), (wr_sb, wr_d)]:
            inc(g.dma_start(out=dst[:], in_=src), "dCG", 16)

    cCY, cCG = cnt["dCY"], cnt["dCG"]

    @blk.vector
    def _(v):
        wt(v, "V", "dCY", cCY)
        wt(v, "V", "dCG", cCG)
        inc(v.memset(ones128[:], 1.0), "cV")
        wt(v, "V", "cV", cnt["cV"])
        inc(v.memset(eps_sb[:], 1e-5), "cV")
        wt(v, "V", "cV", cnt["cV"])
        inc(v.memset(eps12_sb[:], 1e-12), "cV")

    PV = 7
    PS = 2 + 4

    def emit_ln(src_d, dst_dram, base):
        ph1_cV0 = cnt["cV"]
        ph1_cS0 = cnt["cS"]
        ph1_mT0 = cnt["mT"]
        xld = lambda g: 16 * (base * (NG // 2) + g // 2 + 1)
        gld = lambda g: 16 * (base * (NG // 2) + g // 2 + 1)

        @blk.sync
        def _(y):
            def spill(g):
                wt(y, "Y", "cS", ph1_cS0 + g * PS + PS)
                ap = bass.AP(dst_dram.tensor, g * 128,
                             [[KCN * TC, 128], [TC, KCN], [1, 128]])
                inc(y.dma_start(out=ap, in_=xn_stage[g % 2][:]),
                    f"dg{g % 2}", 16)

            for g in range(NG):
                if g >= 2 or base:
                    wt(y, "Y", "cV", ph1_cV0 + (g - 2) * PV + 7)
                    wt(y, "Y", "cS", ph1_cS0 + (g - 2) * PS + 1)
                inc(y.dma_start(out=x_tm[g % 2][:],
                                in_=src_d[g * 128:(g + 1) * 128, :]),
                    f"dx{g % 2}", 16)
                if g >= 1:
                    spill(g - 1)
            spill(NG - 1)

        @blk.vector
        def _(v):
            for g in range(NG):
                wt(v, "V", f"dx{g % 2}", xld(g))
                xg = x_tm[g % 2]
                wt(v, "V", "cV", cnt["cV"])
                inc(v.reduce_sum(st_s1[:], xg[:], axis=AX.X), "cV")
                wt(v, "V", "cV", cnt["cV"])
                inc(v.tensor_scalar(st_negm[:], st_s1[:], -1.0 / HID, None,
                                    ALU.mult), "cV")
                wt(v, "V", "cS", ph1_cS0 + g * PS + 1)
                wt(v, "V", "cV", cnt["cV"])
                inc(v.tensor_scalar(st_var[:], st_ss[:], 1.0 / HID, None,
                                    ALU.mult), "cV")
                wt(v, "V", "cV", cnt["cV"])
                inc(v.tensor_mul(st_m2[:], st_negm[:], st_negm[:]), "cV")
                wt(v, "V", "cV", cnt["cV"])
                inc(v.tensor_sub(st_var[:], st_var[:], st_m2[:]), "cV")
                wt(v, "V", "cS", ph1_cS0 + g * PS + 2)
                wt(v, "V", "cV", cnt["cV"])
                inc(v.reciprocal(st_rstd[:], st_std[:]), "cV")
                if g >= 1 or base:
                    wt(v, "V", "mT", ph1_mT0 + (g - 1) * 16 + 16)
                wt(v, "V", "cV", cnt["cV"])
                inc(v.tensor_scalar(xn_tm[:], xg[:], st_negm[:], st_rstd[:],
                                    ALU.add, ALU.mult), "cV")


        @blk.scalar
        def _(s):
            for g in range(NG):
                wt(s, "S", f"dx{g % 2}", xld(g))
                if g >= 1 or base:
                    wt(s, "S", "cV", ph1_cV0 + (g - 1) * PV + 3)
                wt(s, "S", "cS", cnt["cS"])
                inc(s.activation(out=xsq[:], in_=x_tm[g % 2][:],
                                 func=AF.Square, accum_out=st_ss[:]), "cS")
                wt(s, "S", "cV", ph1_cV0 + g * PV + 5)
                wt(s, "S", "cS", cnt["cS"])
                inc(s.activation(out=st_std[:], in_=st_var[:], func=AF.Sqrt,
                                 bias=eps_sb[:], scale=1.0), "cS")
                if g >= 2 or base:
                    wt(s, "S", f"dg{g % 2}", gld(g - 2))
                for q4 in range(4):
                    wt(s, "S", "mT", ph1_mT0 + g * 16 + q4 * 4 + 4)
                    wt(s, "S", "cS", cnt["cS"])
                    inc(s.activation(out=xn_stage[g % 2][:, q4 * 512:
                                                         (q4 + 1) * 512],
                                     in_=pbh[2 + q4 % 2][:, 0:512],
                                     func=AF.Copy, bias=0.0, scale=1.0), "cS")

        @blk.tensor
        def _(t):
            wt(t, "T", "dCY", cCY)
            for g in range(NG):
                wt(t, "T", "cV", ph1_cV0 + g * PV + 7)
                for c in range(KCN):
                    q4, sl = divmod(c, 4)
                    done4 = g * 4 + q4 - 2
                    if (done4 >= 0 or base) and sl == 0:
                        wt(t, "T", "cS",
                           ph1_cS0 + (done4 // 4) * PS + 2 + (done4 % 4) + 1)
                    inc(t.transpose(pbh[2 + q4 % 2][:, sl * 128:
                                                    (sl + 1) * 128],
                                    xn_tm[:, c * 128:(c + 1) * 128],
                                    id_sb[:]), "mT")

    emit_ln(x_d, xn_dram, 0)
    emit_ln(xs_d, xns_dram, 1)

    ph1_ctx.close()
    ph2_ctx = ExitStack()
    E2 = ph2_ctx.enter_context
    sb2 = lambda n, s, d: E2(nc.sbuf_tensor(n, s, d))
    xn_st = [sb2("xn_st0", [128, KCN * ST], BF16)] * 2
    wbuf = [sb2(f"wbuf{i}", [128, 2048], BF16) for i in range(3)]
    q_f = [sb2(f"q_f{i}", [128, H * ST], BF16) for i in range(2)]
    k_f = [sb2(f"k_f{i}", [128, HKV * ST], BF16) for i in range(2)]
    v_gf = [sb2(f"v_gf{i}", [128, HKV * ST], BF16) for i in range(2)]
    cos_sb = sb2("cos_sb", [128, TC], F32)
    sin_sb = sb2("sin_sb", [128, TC], F32)
    fq = sb2("fq", [128, STG * 256], BF16)
    fk = sb2("fk", [128, STG * 128], BF16)
    qr_b = [sb2(f"qr_b{i}", [128, ST], BF16) for i in range(4)]
    sq_b = [sb2(f"sq_b{i}", [128, ST], BF16) for i in range(4)]
    un_b = [sb2(f"un_b{i}", [128, ST], BF16) for i in range(4)]
    nsq_b = [sb2(f"nsq_b{i}", [128, ST], F32) for i in range(4)]
    invn_b = nsq_b
    rrA = [sb2(f"rrA{i}", [128, ST], F32) for i in range(4)]
    parg_b = rrA
    v16_b = [sb2(f"v16_b{i}", [128, 128], BF16) for i in range(2)]
    am_b = [sb2(f"am_b{i}", [128, 256], BF16) for i in range(2)]
    attn_st = sb2("attn_st", [128, H * ST], BF16)
    s_st = sb2("s_st", [128, MCN * 512], BF16)
    woin_b = [q_f[0], attn_st]

    def _barrier(bar):
        for dec, ekey in [(blk.tensor, "T"), (blk.vector, "V"),
                          (blk.scalar, "S"), (blk.gpsimd, "G"),
                          (blk.sync, "Y")]:
            def _f(eng, ekey=ekey, bar=bar):
                own = {"T": "mT", "V": "cV", "S": "cS"}.get(ekey)
                for s, c in bar.items():
                    if s != own and c > 0:
                        wt(eng, ekey, s, c)
            dec(_f)

    @blk.gpsimd
    def _(g):
        for s_, c_ in list(cnt.items()):
            wt(g, "G", s_, c_)
        inc(g.dma_start(out=cos_sb[:], in_=cos_d), "dCG", 16)
        inc(g.dma_start(out=sin_sb[:], in_=sin_d), "dCG", 16)

    _barrier(dict(cnt))

    NOC = H + HKV + HKV
    qkvsrc = ([(wq_d, oc) for oc in range(16)] + [(wk_d, oc) for oc in range(8)]
              + [(wv_d, oc) for oc in range(8)])
    caps = {}
    NV = 5
    PP = 5
    cv16 = lambda g: 2 * g + 1
    camk = lambda g: 2 * g + 2

    def ev_at(st, j):
        if st == 0:
            return caps["ev0"] + j + 1
        C = caps[st - 1]
        return C["S0"] + C["Spos"][("evq", j)]

    def qkvT_at(st, j):
        if st == 0:
            return caps["qkv0"] + j + 1
        C = caps[st - 1]
        return C["T0"] + C["Tpos"][("qkv", j)]

    @blk.sync
    def _(y):
        ap = bass.AP(xn_dram.tensor, 0, [[KCN * TC, 128], [TC, KCN], [1, ST]])
        inc(y.dma_start(out=xn_st[0][:], in_=ap), "dn0", 16)
    xn_ld0 = ("dn0", cnt["dn0"])

    caps["qkv0"] = cnt["mT"]
    caps["ev0"] = cnt["cS"]
    wld0 = []

    @blk.gpsimd
    def _(g):
        for j in range(NOC):
            if j >= 3:
                wt(g, "G", "mT", qkvT_at(0, j - 3))
            sname = f"dw{j % 3}"
            inc(g.dma_start(out=wbuf[j % 3][:],
                            in_=qkvsrc[j][0][qkvsrc[j][1]]), sname, 16)
            wld0.append((sname, cnt[sname]))

    @blk.tensor
    def _(t):
        wt(t, "T", *xn_ld0)
        for j in range(NOC):
            wt(t, "T", *wld0[j])
            if j >= 2:
                wt(t, "T", "cS", ev_at(0, j - 2))
            else:
                wt(t, "T", "cS", caps["ev0"])
            for kc in range(KCN):
                ins = t.matmul(pb[j % 2][:, 0:ST],
                               wbuf[j % 3][:, kc * 128:(kc + 1) * 128],
                               xn_st[0][:, kc * ST:(kc + 1) * ST],
                               start=(kc == 0), stop=(kc == KCN - 1))
            inc(ins, "mT")

    def emit_qkv_evac(s, st, j):
        if j < 16:
            dst = q_f[st % 2][:, j * ST:(j + 1) * ST]
            bias = bq_sb[:, j:j + 1]
        elif j < 24:
            hk = j - 16
            dst = k_f[st % 2][:, hk * ST:(hk + 1) * ST]
            bias = bk_sb[:, hk:hk + 1]
        else:
            hk = j - 24
            dst = bass.AP(v_gf[st % 2], hk,
                          [[HKV * ST, 128], [128, STG], [8, 16]])
            bias = bv_sb[:, hk:hk + 1]
        inc(s.activation(out=dst, in_=pb[j % 2][:, 0:ST], func=AF.Identity,
                         bias=bias, scale=1.0), "cS")

    @blk.scalar
    def _(s):
        for j in range(NOC):
            wt(s, "S", "mT", qkvT_at(0, j))
            wt(s, "S", "cS", cnt["cS"])
            emit_qkv_evac(s, 0, j)

    for st in range(NST):
        nxt = st + 1 < NST
        C = caps.setdefault(st, {})
        C["T0"] = T0 = cnt["mT"]
        C["S0"] = S0 = cnt["cS"]
        C["G0"] = G0 = cnt["cV"]

        t_ops = []
        for hh in range(NHEADS + 3):
            if hh >= 3:
                t_ops.append(("phi", hh - 3))
            if hh < NHEADS:
                t_ops.append(("nsq", hh))
            if nxt and 2 <= hh < NHEADS + 2:
                t_ops.append(("qkv", hh - 2))
            if hh < NHEADS:
                t_ops.append(("pq", hh))
        if nxt:
            for j in range(NHEADS, NOC):
                t_ops.append(("qkv", j))
        Tpos = {op: i + 1 for i, op in enumerate(t_ops)}
        C["Tpos"] = Tpos
        co_mT0 = T0 + len(t_ops)

        s_ops = []
        for g in range(NHEADS + 4):
            if 1 <= g <= NHEADS:
                s_ops.append(("rsq", g - 1))
            if 3 <= g < NHEADS + 3:
                s_ops.append(("rr1", g - 3))
            if nxt and 2 <= g < NHEADS + 2:
                s_ops.append(("evq", g - 2))
            if 4 <= g < NHEADS + 4:
                s_ops.append(("sin", g - 4))
        if nxt:
            for j in range(NHEADS, NOC):
                s_ops.append(("evq", j))
        Spos = {op: i + 1 for i, op in enumerate(s_ops)}
        C["Spos"] = Spos

        v_ops = []
        for sl in range(NHEADS + 2):
            if sl < NHEADS:
                v_ops.append(("vtmp", sl))
                v_ops.append(("vadd", sl))
                v_ops.append(("vrcp", sl))
                v_ops.append(("vunit", sl))
            if sl >= 2:
                v_ops.append(("vrr2", sl - 2))
                v_ops.append(("vrr3", sl - 2))
        Vpos = {op: i + 1 for i, op in enumerate(v_ops)}
        C["Vpos"] = Vpos

        C["P0"] = P0 = cnt["cP"]
        C["glue_cV_end"] = G0 + len(v_ops)
        C["glue_cP_end"] = P0 + 2 * NHEADS
        coV0 = G0 + len(v_ops)
        coS0 = S0 + len(s_ops)

        if nxt:
            @blk.sync
            def _(y, st=st):
                wt(y, "Y", "mT", qkvT_at(st, NOC - 1))
                ap = bass.AP(xn_dram.tensor, (st + 1) * ST,
                             [[KCN * TC, 128], [TC, KCN], [1, ST]])
                inc(y.dma_start(out=xn_st[(st + 1) % 2][:], in_=ap),
                    f"dn{(st + 1) % 2}", 16)
            C["xn_ld"] = (f"dn{(st + 1) % 2}", cnt[f"dn{(st + 1) % 2}"])


        wldn = []
        C["wld"] = wldn

        @blk.gpsimd
        def _(g, st=st, nxt=nxt, T0=T0, S0=S0, G0=G0, P0=P0, Tpos=Tpos,
              Spos=Spos, Vpos=Vpos, wldn=wldn):
            c_ = cos_sb[:, st * ST:(st + 1) * ST]

            def dw(j):
                if j >= 3:
                    wt(g, "G", "mT", qkvT_at(st + 1, j - 3))
                else:
                    wt(g, "G", "mT", qkvT_at(st, NOC - 1))
                sname = f"dw{j % 3}"
                inc(g.dma_start(out=wbuf[j % 3][:],
                                in_=qkvsrc[j][0][qkvsrc[j][1]]), sname, 16)
                wldn.append((sname, cnt[sname]))

            def pops(k):
                isq = k < H
                srcT = (q_f if isq else k_f)[st % 2]
                base = (k if isq else k - H) * ST
                src_ = srcT[:, base:base + ST]
                wt(g, "G", "cS", ev_at(st, k))
                if k >= 4:
                    wt(g, "G", "cV", G0 + Vpos[("vadd", k - 4)])
                elif st >= 1:
                    wt(g, "G", "cV", caps[st - 1]["glue_cV_end"])
                wt(g, "G", "cP", cnt["cP"])
                inc(g.tensor_mul(sq_b[k % 4][:], src_, src_), "cP")
                if k >= 4:
                    wt(g, "G", "cV", G0 + Vpos[("vunit", k - 4)])
                wt(g, "G", "cP", cnt["cP"])
                inc(g.tensor_mul(qr_b[k % 4][:], src_, c_), "cP")

            if nxt:
                for j in range(3):
                    dw(j)
            pops(0)
            pops(1)
            for bb in range(NHEADS):
                if bb + 2 < NHEADS:
                    pops(bb + 2)
                if nxt and 3 <= bb + 2 < NOC:
                    dw(bb + 2)
            if nxt:
                for j in range(NHEADS + 2, NOC):
                    dw(j)

        @blk.scalar
        def _(s, st=st, nxt=nxt, T0=T0, S0=S0, G0=G0, Tpos=Tpos, Spos=Spos,
              Vpos=Vpos, s_ops=s_ops):
            if st >= 1:
                wt(s, "S", "mT", caps[st - 1]["core_mT_end"])
                wt(s, "S", "cV", caps[st - 1]["glue_cV_end"])
                wt(s, "S", "cP", caps[st - 1]["glue_cP_end"])
            for op in s_ops:
                kind, k = op
                if kind == "rsq":
                    wt(s, "S", "mT", T0 + Tpos[("nsq", k)])
                    if k >= 4:
                        wt(s, "S", "cV", G0 + Vpos[("vunit", k - 4)])
                    wt(s, "S", "cS", cnt["cS"])
                    inc(s.activation(out=nsq_b[k % 4][:],
                                     in_=pb[2 + k % 2][:, 0:ST],
                                     func=AF.Sqrt, bias=eps12_sb[:],
                                     scale=1.0), "cS")
                elif kind == "rr1":
                    wt(s, "S", "mT", T0 + Tpos[("phi", k)])
                    if k >= 4:
                        wt(s, "S", "cV", G0 + Vpos[("vrr3", k - 4)])
                    wt(s, "S", "cS", cnt["cS"])
                    inc(s.activation(out=rrA[k % 4][:],
                                     in_=pb[6 + k % 2][:, 0:ST],
                                     func=AF.Identity, bias=offr_sb[:],
                                     scale=INV_2PI), "cS")
                elif kind == "sin":
                    wt(s, "S", "cV", G0 + Vpos[("vrr3", k)])
                    wt(s, "S", "cS", cnt["cS"])
                    if k < H:
                        dst = bass.AP(fq, k * 16,
                                      [[STG * 256, 128], [256, STG], [1, 16]])
                    else:
                        dst = bass.AP(fk, k - H,
                                      [[STG * 128, 128], [128, STG], [8, 16]])
                    inc(s.activation(out=dst, in_=parg_b[k % 4][:],
                                     func=AF.Sin, bias=offs_sb[:], scale=1.0),
                        "cS")
                else:
                    wt(s, "S", "mT", T0 + Tpos[("qkv", k)])
                    wt(s, "S", "cS", cnt["cS"])
                    emit_qkv_evac(s, st + 1, k)

        @blk.vector
        def _(v, st=st, T0=T0, S0=S0, P0=P0, Tpos=Tpos, Spos=Spos,
              v_ops=v_ops):
            s_ = sin_sb[:, st * ST:(st + 1) * ST]
            for op in v_ops:
                kind, k = op
                if kind == "vtmp":
                    wt(v, "V", "mT", T0 + Tpos[("pq", k)])
                    wt(v, "V", "cV", cnt["cV"])
                    inc(v.tensor_mul(sq_b[k % 4][:], pb[4 + k % 2][:, 0:ST],
                                     s_), "cV")
                elif kind == "vadd":
                    wt(v, "V", "cP", P0 + 2 * k + 2)
                    wt(v, "V", "cV", cnt["cV"])
                    inc(v.tensor_add(qr_b[k % 4][:], qr_b[k % 4][:],
                                     sq_b[k % 4][:]), "cV")
                elif kind == "vrcp":
                    wt(v, "V", "cS", S0 + Spos[("rsq", k)])
                    wt(v, "V", "cV", cnt["cV"])
                    inc(v.reciprocal(invn_b[k % 4][:], nsq_b[k % 4][:]), "cV")
                elif kind == "vunit":
                    if k >= 4:
                        wt(v, "V", "mT", T0 + Tpos[("phi", k - 4)])
                    wt(v, "V", "cV", cnt["cV"])
                    inc(v.tensor_mul(un_b[k % 4][:], qr_b[k % 4][:],
                                     invn_b[k % 4][:]), "cV")
                elif kind == "vrr2":
                    wt(v, "V", "cS", S0 + Spos[("rr1", k)])
                    wt(v, "V", "cV", cnt["cV"])
                    inc(v.tensor_scalar(rrA[k % 4][:], rrA[k % 4][:], MAGIC,
                                        MAGIC, ALU.add, ALU.subtract), "cV")
                else:
                    if k >= 4:
                        wt(v, "V", "cS", S0 + Spos[("sin", k - 4)])
                    wt(v, "V", "cV", cnt["cV"])
                    inc(v.scalar_tensor_tensor(parg_b[k % 4][:], rrA[k % 4][:],
                                               -TWO_PI, pb[6 + k % 2][:, 0:ST],
                                               ALU.mult, ALU.add), "cV")

        @blk.tensor
        def _(t, st=st, nxt=nxt, T0=T0, S0=S0, G0=G0, P0=P0, Tpos=Tpos,
              Spos=Spos, Vpos=Vpos, t_ops=t_ops, co_mT0=co_mT0,
              coV0=coV0, C=C):
            wt(t, "T", "dCG", cCG)
            for op in t_ops:
                kind, k = op
                if kind == "pq":
                    isq = k < H
                    srcT = (q_f if isq else k_f)[st % 2]
                    base = (k if isq else k - H) * ST
                    wt(t, "T", "cS", ev_at(st, k))
                    if k >= 2:
                        wt(t, "T", "cV", G0 + Vpos[("vtmp", k - 2)])
                    elif st >= 1:
                        wt(t, "T", "cV", caps[st - 1]["coreV_end"])
                    inc(t.matmul(pb[4 + k % 2][:, 0:ST], pm_sb[:],
                                 srcT[:, base:base + ST], start=True,
                                 stop=True), "mT")
                elif kind == "nsq":
                    wt(t, "T", "cP", P0 + 2 * k + 1)
                    if k >= 2:
                        wt(t, "T", "cS", S0 + Spos[("rsq", k - 2)])
                    elif st >= 1:
                        wt(t, "T", "cV", caps[st - 1]["coreV_end"])
                    inc(t.matmul(pb[2 + k % 2][:, 0:ST], ones128[:],
                                 sq_b[k % 4][:], start=True, stop=True), "mT")
                elif kind == "phi":
                    wt(t, "T", "cV", G0 + Vpos[("vunit", k)])
                    if k >= 2:
                        wt(t, "T", "cV", G0 + Vpos[("vrr3", k - 2)])
                    elif st >= 1:
                        wt(t, "T", "cV", caps[st - 1]["coreV_end"])
                    t.matmul(pb[6 + k % 2][0:64, 0:ST], rt_sb[:],
                             un_b[k % 4][:], start=True, stop=True)
                    inc(t.matmul(pb[6 + k % 2][64:128, 0:ST], rt_sb[:],
                                 un_b[k % 4][:], start=True, stop=True), "mT")
                else:
                    j = k
                    wt(t, "T", *C["wld"][j])
                    wt(t, "T", *C["xn_ld"])
                    if j >= 2:
                        wt(t, "T", "cS", ev_at(st + 1, j - 2))
                    else:
                        wt(t, "T", "cS", ev_at(st, NOC - 1))
                    for kc in range(KCN):
                        ins = t.matmul(pb[j % 2][:, 0:ST],
                                       wbuf[j % 3][:, kc * 128:(kc + 1) * 128],
                                       xn_st[(st + 1) % 2][:, kc * ST:
                                                           (kc + 1) * ST],
                                       start=(kc == 0), stop=(kc == KCN - 1))
                    inc(ins, "mT")
            wt(t, "T", "cS", S0 + Spos[("sin", NHEADS - 1)])
            for gg in range(STG):
                if gg >= 2:
                    wt(t, "T", "cV", coV0 + cv16(gg - 2))
                else:
                    wt(t, "T", "cV", C["glue_cV_end"])
                inc(t.transpose(pbh[2 + gg % 2][:, 0:128],
                                v_gf[st % 2][:, gg * 128:(gg + 1) * 128],
                                id_sb[:]), "mT")
                if gg >= 2:
                    wt(t, "T", "cV", coV0 + camk(gg - 2))
                inc(t.matmul(pb[4 + gg % 2][:, 0:256],
                             fk[:, gg * 128:(gg + 1) * 128],
                             fq[:, gg * 256:(gg + 1) * 256],
                             start=True, stop=True), "mT")
                if gg >= 1:
                    p = gg - 1
                    wt(t, "T", "cV", coV0 + camk(p))
                    if p >= 2:
                        wt(t, "T", "cS", coS0 + (p - 2) + 1)
                    inc(t.matmul(pb[6 + p % 2][:, 0:256], v16_b[p % 2][:],
                                 am_b[p % 2][:], start=True, stop=True), "mT")
            p = STG - 1
            wt(t, "T", "cV", coV0 + camk(p))
            if p >= 2:
                wt(t, "T", "cS", coS0 + (p - 2) + 1)
            inc(t.matmul(pb[6 + p % 2][:, 0:256], v16_b[p % 2][:],
                         am_b[p % 2][:], start=True, stop=True), "mT")
        C["core_mT_end"] = cnt["mT"]

        @blk.vector
        def _(v, st=st, co_mT0=co_mT0, C=C):
            for gg in range(STG):
                wt(v, "V", "mT", co_mT0 + gg * 3 + 1)
                wt(v, "V", "cV", cnt["cV"])
                inc(v.tensor_copy(v16_b[gg % 2][:], pbh[2 + gg % 2][:, 0:128]),
                    "cV")
                wt(v, "V", "mT", co_mT0 + gg * 3 + 2)
                wt(v, "V", "cV", cnt["cV"])
                inc(v.tensor_mul(am_b[gg % 2][:], pb[4 + gg % 2][:, 0:256],
                                 mask_sb[:]), "cV")
        C["coreV_end"] = cnt["cV"]

        @blk.scalar
        def _(s, st=st, co_mT0=co_mT0, C=C, _at=cnt["dat"]):
            if st >= 1:
                wt(s, "S", "dat", _at)

            def _evdst(p):
                return bass.AP(attn_st, p, [[H * ST, 128], [ST, H], [SGC, 16]])

            for p in range(STG):
                if p < STG - 1:
                    wt(s, "S", "mT", co_mT0 + (p + 1) * 3 + 3)
                else:
                    wt(s, "S", "mT", C["core_mT_end"])
                wt(s, "S", "cS", cnt["cS"])
                inc(s.activation(out=_evdst(p), in_=pb[6 + p % 2][:, 0:256],
                                 func=AF.Copy, bias=0.0, scale=1.0), "cS")
        C["coreS_end"] = cnt["cS"]

        @blk.sync
        def _(y, st=st, _ce=C["coreS_end"]):
            wt(y, "Y", "cS", _ce)
            for h in range(16):
                dst = bass.AP(satt_dram.tensor, h * TC + st * SGC,
                              [[16 * TC, 128], [TC // 16, 16], [1, SGC]])
                src = bass.AP(attn_st, h * ST,
                              [[H * ST, 128], [SGC, 16], [1, SGC]])
                inc(y.dma_start(out=dst, in_=src), "dat", 16)

    att_done = cnt["dat"]
    HIC2 = 512 // (TC // 16)

    for tc2 in range(NTC2):
        wo_mT0 = cnt["mT"]
        wo_cS0 = cnt["cS"]
        wld2 = []
        wil = []

        @blk.sync
        def _(y, tc2=tc2, wo_mT0=wo_mT0):
            wt(y, "Y", "dat", att_done)
            if tc2 >= 2:
                wt(y, "Y", "mT", caps[f"wo{tc2 - 2}"])
            for half in range(2):
                sname = f"dwi{half}"
                for j in range(half * 8, half * 8 + 8):
                    dims = [[16 * TC, 128]]
                    if HIC2 > 1:
                        dims.append([TC, HIC2])
                    dims.append([1, TC // 16])
                    src = bass.AP(satt_dram.tensor,
                                  tc2 * HIC2 * TC + j * (TC // 16), dims)
                    inc(y.dma_start(
                        out=woin_b[tc2 % 2][:, j * 512:(j + 1) * 512],
                        in_=src), sname, 16)
                    wil.append((sname, cnt[sname]))

        @blk.gpsimd
        def _(g, tc2=tc2, wo_mT0=wo_mT0):
            for oc in range(16):
                if oc >= 3:
                    wt(g, "G", "mT", wo_mT0 + (oc - 3) + 1)
                elif tc2 >= 1:
                    wt(g, "G", "mT", caps[f"wo{tc2 - 1}"])
                else:
                    wt(g, "G", "mT", qkvT_at(NST - 1, NOC - 1))
                sname = f"dw{oc % 3}"
                inc(g.dma_start(out=wbuf[oc % 3][:], in_=wo_d[oc]), sname, 16)
                wld2.append((sname, cnt[sname]))

        @blk.tensor
        def _(t, tc2=tc2, wo_cS0=wo_cS0, wld2=wld2, wil=wil):
            wt(t, "T", *wil[7])
            for oc in range(16):
                wt(t, "T", *wld2[oc])
                if oc >= 2:
                    wt(t, "T", "cS", wo_cS0 + (oc - 2) + 1)
                else:
                    wt(t, "T", "cS", wo_cS0)
                for kc in range(KCN):
                    if oc == 0 and kc == 8:
                        wt(t, "T", *wil[-1])
                    ins = t.matmul(pb[oc % 2][:, 0:512],
                                   wbuf[oc % 3][:, kc * 128:(kc + 1) * 128],
                                   woin_b[tc2 % 2][:, kc * 512:(kc + 1) * 512],
                                   start=(kc == 0), stop=(kc == KCN - 1))
                inc(ins, "mT")
        caps[f"wo{tc2}"] = cnt["mT"]

        @blk.scalar
        def _(s, tc2=tc2, wo_mT0=wo_mT0, _sp=cnt["dsp"]):
            if tc2 >= 1:
                wt(s, "S", "dsp", _sp)
            for oc in range(16):
                wt(s, "S", "mT", wo_mT0 + oc + 1)
                wt(s, "S", "cS", cnt["cS"])
                inc(s.activation(out=s_st[:, oc * 512:(oc + 1) * 512],
                                 in_=pb[oc % 2][:, 0:512], func=AF.Identity,
                                 bias=bo_sb[:, oc:oc + 1], scale=1.0), "cS")

        s_cS_end = cnt["cS"]

        @blk.sync
        def _(y, tc2=tc2, s_cS_end=s_cS_end):
            wt(y, "Y", "cS", s_cS_end)
            ap = bass.AP(s_dram.tensor, tc2 * 512,
                         [[MCN * TC, 128], [TC, MCN], [1, 512]])
            inc(y.dma_start(out=ap, in_=s_st[:]), "dsp", 16)

    ph2_ctx.close()
    ph3_ctx = ExitStack()
    E3 = ph3_ctx.enter_context
    sb3 = lambda n, s, d: E3(nc.sbuf_tensor(n, s, d))
    xn_tb = sb3("xn_tb", [128, KCN * TB], BF16)
    ones1 = sb3("ones1", [65, 128], BF16)
    acc_tb = sb3("acc_tb", [128, MCN * TB], F32)
    mids = sb3("mids", [128, EG * DCN * TB], BF16)
    wup_b = [sb3(f"wup_b{i}", [128, 2048], BF16) for i in range(2)]
    wdn_b = [sb3(f"wdn_b{i}", [128, 1024], BF16) for i in range(4)]
    gate_tm = sb3("gate_tm", [128, 128], BF16)
    ggf = sb3("ggf", [128, 128], BF16)
    grow_t = sb3("grow_t", [65, TB], BF16)
    grow = [grow_t[32 * i:32 * i + 1] for i in range(EG)]
    r_es = sb3("r_es", [128, 16], F32)
    r_pr = sb3("r_pr", [128, 16], F32)
    r_t1 = sb3("r_t1", [128, 16], F32)
    r_m = sb3("r_m", [128, 1], F32)
    r_s = sb3("r_s", [128, 1], F32)
    sgb_b = [sb3(f"sgb_b{i}", [128, 128], BF16) for i in range(2)]
    g_sb = [sb3(f"g_sb{i}", [128, 128], BF16) for i in range(2)]
    xb_b = sb3("xb_b", [128, HID], F32)
    of_b = [sb3(f"of_b{i}", [128, HID], F32) for i in range(2)]
    s_u = [sb3(f"s_u{i}", [128, HID], BF16) for i in range(4)]

    _barrier(dict(cnt))

    @blk.vector
    def _(v):
        wt(v, "V", "cV", cnt["cV"])
        inc(v.memset(ones1[:], 1.0), "cV")

    RV = 10
    rtc = {}

    def emit_rt_mm(tb, bank_cV):
        @blk.sync
        def _(y, tb=tb):
            if tb == 0:
                wt(y, "Y", "cV", cnt["cV"])
                wt(y, "Y", "cS", cnt["cS"])
            else:
                wt(y, "Y", "mT", caps[f"upend{tb - 1}"])
            ap = bass.AP(xns_dram.tensor, tb * TB,
                         [[KCN * TC, 128], [TC, KCN], [1, TB]])
            inc(y.dma_start(out=xn_tb[:], in_=ap), "dxt", 16)
        rtc[tb] = R = {}
        R["xnt_ld"] = cnt["dxt"]
        R["mT0"] = cnt["mT"]

        @blk.tensor
        def _(t, tb=tb, bank_cV=bank_cV):
            wt(t, "T", "dxt", rtc[tb]["xnt_ld"])
            if bank_cV is not None:
                wt(t, "T", "cV", bank_cV)
            for g in range(GTB):
                for kc in range(KCN):
                    ins = t.matmul(pb[4][:, g * 16:(g + 1) * 16],
                                   xn_tb[:, kc * TB + g * 128:
                                         kc * TB + (g + 1) * 128],
                                   wr_sb[:, kc * 16:(kc + 1) * 16],
                                   start=(kc == 0), stop=(kc == KCN - 1))
                inc(ins, "mT")

    def emit_rt_rest(tb):
        R = rtc[tb]
        rt_mT0 = R["mT0"]
        rt_cV0 = cnt["cV"]
        rt_cS0 = cnt["cS"]
        R["cV0"], R["cS0"] = rt_cV0, rt_cS0

        @blk.vector
        def _(v, tb=tb, rt_mT0=rt_mT0, rt_cS0=rt_cS0):
            wt(v, "V", "cS", rt_cS0)
            for g in range(GTB):
                psl = pb[4][:, g * 16:(g + 1) * 16]
                wt(v, "V", "mT", rt_mT0 + GTB)
                wt(v, "V", "cV", cnt["cV"])
                inc(v.reduce_max(r_m[:], psl, axis=AX.X, negate=True), "cV")
                wt(v, "V", "cS", rt_cS0 + g + 1)
                wt(v, "V", "cV", cnt["cV"])
                inc(v.reduce_sum(r_s[:], r_es[:], axis=AX.X), "cV")
                wt(v, "V", "cV", cnt["cV"])
                inc(v.reciprocal(r_s[:], r_s[:]), "cV")
                wt(v, "V", "cV", cnt["cV"])
                inc(v.tensor_scalar(r_pr[:], r_es[:], r_s[:], None, ALU.mult),
                    "cV")
                wt(v, "V", "cV", cnt["cV"])
                inc(v.reduce_max(r_m[:], r_pr[:], axis=AX.X), "cV")
                wt(v, "V", "cV", cnt["cV"])
                inc(v.tensor_single_scalar(r_t1[:], r_pr[:], r_m[:],
                                           ALU.is_equal), "cV")
                wt(v, "V", "cV", cnt["cV"])
                inc(v.scalar_tensor_tensor(r_t1[:], r_t1[:], -1e30, r_pr[:],
                                           ALU.mult, ALU.add), "cV")
                wt(v, "V", "cV", cnt["cV"])
                inc(v.reduce_max(r_m[:], r_t1[:], axis=AX.X), "cV")
                wt(v, "V", "cV", cnt["cV"])
                inc(v.tensor_single_scalar(r_t1[:], r_pr[:], r_m[:],
                                           ALU.is_ge), "cV")
                gdst = bass.AP(gate_tm, g, [[128, 128], [GTB, 16]])
                wt(v, "V", "cV", cnt["cV"])
                inc(v.tensor_mul(gdst, r_t1[:], r_pr[:]), "cV")

        @blk.scalar
        def _(s, tb=tb, rt_mT0=rt_mT0, rt_cV0=rt_cV0):
            for g in range(GTB):
                psl = pb[4][:, g * 16:(g + 1) * 16]
                wt(s, "S", "mT", rt_mT0 + GTB)
                wt(s, "S", "cV", rt_cV0 + g * RV + 1)
                if g >= 1:
                    wt(s, "S", "cV", rt_cV0 + (g - 1) * RV + 4)
                wt(s, "S", "cS", cnt["cS"])
                inc(s.activation(out=r_es[:], in_=psl, func=AF.Exp,
                                 bias=r_m[:], scale=1.0), "cS")
        R["gates_cV"] = rt_cV0 + GTB * RV

    emit_rt_mm(0, None)
    for tb in range(NTB):
        emit_rt_rest(tb)
        R = rtc[tb]
        prev_gbc_mT = [None]

        for eg in range(NEG):
            up_mT0 = cnt["mT"]
            up_cS0 = cnt["cS"]

            uld = []

            @blk.gpsimd
            def _(g, tb=tb, eg=eg, up_mT0=up_mT0):
                for i in range(EG * DCN):
                    e3, dc = divmod(i, DCN)
                    e = eg * EG + e3
                    if i >= 2:
                        wt(g, "G", "mT", up_mT0 + (i - 2) * NCH + NCH)
                    else:
                        wt(g, "G", "mT", up_mT0)
                    sname = f"du{i % 2}"
                    inc(g.dma_start(out=wup_b[i % 2][:], in_=wup_d[e, dc]),
                        sname, 16)
                    uld.append((sname, cnt[sname]))

            @blk.tensor
            def _(t, tb=tb, eg=eg, up_cS0=up_cS0, uld=uld):
                for i in range(EG * DCN):
                    wt(t, "T", *uld[i])
                    for n in range(NCH):
                        k = i * NCH + n
                        if k >= 2:
                            wt(t, "T", "cS", up_cS0 + k - 1)
                        else:
                            wt(t, "T", "cS", up_cS0)
                        for kc in range(KCN):
                            ins = t.matmul(
                                pb[k % 2][:, 0:CH],
                                wup_b[i % 2][:, kc * 128:(kc + 1) * 128],
                                xn_tb[:, kc * TB + n * CH:kc * TB + (n + 1) * CH],
                                start=(kc == 0), stop=(kc == KCN - 1))
                        inc(ins, "mT")
            if eg == NEG - 1:
                caps[f"upend{tb}"] = cnt["mT"]

            if eg == 0:
                tp_at = cnt["mT"] + 1

                @blk.tensor
                def _(t, tb=tb, R=R):
                    wt(t, "T", "cV", R["gates_cV"])
                    inc(t.transpose(pbh[2][0:GTB * 16, 0:128],
                                    gate_tm[:, 0:GTB * 16], id_sb[:]), "mT")

                @blk.vector
                def _(v, tb=tb, tp_at=tp_at):
                    wt(v, "V", "dgr", cnt["dgr"])
                    wt(v, "V", "cV", cnt["cV"])
                    inc(v.memset(ggf[:], 0.0), "cV")
                    wt(v, "V", "mT", tp_at)
                    wt(v, "V", "cV", cnt["cV"])
                    inc(v.tensor_copy(ggf[0:GTB * 16, :],
                                      pbh[2][0:GTB * 16, 0:128]), "cV")
                R["gate_cV"] = cnt["cV"]

            @blk.sync
            def _(y, tb=tb, eg=eg, R=R):
                wt(y, "Y", "cV", R["gate_cV"])
                if prev_gbc_mT[0] is not None:
                    wt(y, "Y", "mT", prev_gbc_mT[0])
                for e3 in range(EG):
                    e = eg * EG + e3
                    if e < NSH:
                        continue
                    er = e - NSH
                    src = ggf[er * GTB:(er + 1) * GTB, 0:128]
                    inc(y.dma_start(out=grow[e3][0:1, :], in_=src), "dgr", 16)
            grow_dS = cnt["dgr"]

            @blk.scalar
            def _(s, tb=tb, eg=eg, up_mT0=up_mT0):
                for i in range(EG * DCN):
                    e3, dc = divmod(i, DCN)
                    for n in range(NCH):
                        k = i * NCH + n
                        wt(s, "S", "mT", up_mT0 + k + 1)
                        wt(s, "S", "cS", cnt["cS"])
                        dst = mids[:, e3 * DCN * TB + dc * TB + n * CH:
                                   e3 * DCN * TB + dc * TB + (n + 1) * CH]
                        inc(s.activation(out=dst, in_=pb[k % 2][:, 0:CH],
                                         func=AF.Gelu), "cS")

            gel_cS = cnt["cS"]
            gm_mT0 = cnt["mT"]
            gm_cV0 = cnt["cV"]

            @blk.tensor
            def _(t, tb=tb, eg=eg, grow_dS=grow_dS, gm_cV0=gm_cV0):
                wt(t, "T", "dgr", grow_dS)
                j = 0
                for e3 in range(EG):
                    e = eg * EG + e3
                    if e < NSH:
                        continue
                    for n in range(NCH):
                        if j >= 2:
                            wt(t, "T", "cV", gm_cV0 + j - 1)
                        inc(t.matmul(pb[2 + j % 2][:, 0:CH],
                                     ones1[32 * e3:32 * e3 + 1],
                                     grow[e3][0:1, n * CH:(n + 1) * CH],
                                     start=True, stop=True), "mT")
                        j += 1
            prev_gbc_mT[0] = cnt["mT"]

            @blk.vector
            def _(v, tb=tb, eg=eg, gm_mT0=gm_mT0, gel_cS=gel_cS):
                j = 0
                for e3 in range(EG):
                    e = eg * EG + e3
                    if e < NSH:
                        continue
                    for n in range(NCH):
                        wt(v, "V", "mT", gm_mT0 + j + 1)
                        wt(v, "V", "cS", gel_cS)
                        src = bass.AP(mids, e3 * DCN * TB + n * CH,
                                      [[EG * DCN * TB, 128], [TB, DCN],
                                       [1, CH]])
                        gb = bass.AP(pb[2 + j % 2], 0,
                                     [[512, 128], [0, DCN], [1, CH]])
                        wt(v, "V", "cV", cnt["cV"])
                        inc(v.tensor_tensor(src, src, gb, ALU.mult), "cV")
                        j += 1

            mid_cV = cnt["cV"]
            dn_mT0 = cnt["mT"]
            dn_cV0 = cnt["cV"]
            dld = []

            @blk.gpsimd
            def _(g, tb=tb, eg=eg, dn_mT0=dn_mT0):
                for i in range(MCN * EG):
                    mc, e3 = divmod(i, EG)
                    e = eg * EG + e3
                    if i >= 4:
                        wt(g, "G", "mT", dn_mT0 + ((i - 4) // EG) * NCH + NCH)
                    else:
                        wt(g, "G", "mT", dn_mT0)
                    sname = f"dd{i % 4}"
                    inc(g.dma_start(out=wdn_b[i % 4][:], in_=wdn_d[e, mc]),
                        sname, 16)
                    dld.append((sname, cnt[sname]))

            @blk.tensor
            def _(t, tb=tb, eg=eg, dn_cV0=dn_cV0, mid_cV=mid_cV,
                  gel_cS=gel_cS, dld=dld):
                wt(t, "T", "cV", mid_cV)
                wt(t, "T", "cS", gel_cS)
                for mc in range(MCN):
                    for e3 in range(EG):
                        wt(t, "T", *dld[mc * EG + e3])
                    for n in range(NCH):
                        idx = mc * NCH + n
                        if idx >= 2:
                            wt(t, "T", "cV", dn_cV0 + idx - 1)
                        else:
                            wt(t, "T", "cV", dn_cV0)
                        for e3 in range(EG):
                            for dc in range(DCN):
                                ins = t.matmul(
                                    pb[4 + idx % 2][:, 0:CH],
                                    wdn_b[(mc * EG + e3) % 4][:,
                                          dc * 128:(dc + 1) * 128],
                                    mids[:, e3 * DCN * TB + dc * TB + n * CH:
                                         e3 * DCN * TB + dc * TB + (n + 1) * CH],
                                    start=(e3 == 0 and dc == 0),
                                    stop=(e3 == EG - 1 and dc == DCN - 1))
                        inc(ins, "mT")

            @blk.vector
            def _(v, tb=tb, eg=eg, dn_mT0=dn_mT0):
                if eg == 0 and tb >= 1:
                    wt(v, "V", "cS", caps[f"siluEnd{tb - 1}"])
                for mc in range(MCN):
                    for n in range(NCH):
                        idx = mc * NCH + n
                        wt(v, "V", "mT", dn_mT0 + idx + 1)
                        dst = acc_tb[:, mc * TB + n * CH:mc * TB + (n + 1) * CH]
                        wt(v, "V", "cV", cnt["cV"])
                        if eg == 0:
                            inc(v.tensor_copy(dst, pb[4 + idx % 2][:, 0:CH]),
                                "cV")
                        else:
                            inc(v.tensor_add(dst, dst,
                                             pb[4 + idx % 2][:, 0:CH]), "cV")

        ffn_cV = cnt["cV"]
        if tb + 1 < NTB:
            emit_rt_mm(tb + 1, ffn_cV)

        cb_cV0 = cnt["cV"]
        cb_mT0 = cnt["mT"]
        cb_cS0 = cnt["cS"]
        cb_d = {0: cnt["dou"], 1: cnt["dou1"]}
        CBV = MCN * 2
        SQS = ["dst", "dst1", "dst2", "dst3"]
        xbl = []
        sul = []
        oul = []

        @blk.scalar
        def _(s, tb=tb, ffn_cV=ffn_cV, cb_cV0=cb_cV0):
            wt(s, "S", "cV", ffn_cV)
            for u in range(GTB):
                for mc in range(MCN):
                    j = u * MCN + mc
                    if j >= 2:
                        uj, mj = divmod(j - 2, MCN)
                        wt(s, "S", "cV", cb_cV0 + uj * CBV + 2 * mj + 1)
                    wt(s, "S", "cS", cnt["cS"])
                    inc(s.activation(
                        out=g_sb[j % 2][:],
                        in_=acc_tb[:, mc * TB + u * 128:
                                   mc * TB + u * 128 + 128],
                        func=AF.Silu), "cS")
        caps[f"siluEnd{tb}"] = cnt["cS"]

        @blk.sync
        def _(y, tb=tb, cb_cV0=cb_cV0, _sp=cnt["dsp"]):
            wt(y, "Y", "dsp", _sp)
            for u in range(GTB):
                if u >= 4:
                    wt(y, "Y", "cV",
                       cb_cV0 + (u - 4) * CBV + 2 * MCN - 1)
                sq_ = SQS[u % 4]
                inc(y.dma_start(out=s_u[u % 4][:],
                                in_=bass.AP(s_dram.tensor, tb * TB + u * 128,
                                            [[MCN * TC, 128], [TC, MCN],
                                             [1, 128]])), sq_, 16)
                sul.append((sq_, cnt[sq_]))
                if u >= 1:
                    wt(y, "Y", "cV", cb_cV0 + (u - 1) * CBV + CBV)
                inc(y.dma_start(out=xb_b[:],
                                in_=xs_d[tb * TB + u * 128:
                                         tb * TB + (u + 1) * 128, :]),
                    "dxb", 16)
                xbl.append(cnt["dxb"])
                if u >= 1:
                    p = u - 1
                    wt(y, "Y", "cV", cb_cV0 + p * CBV + CBV)
                    inc(y.dma_start(out=out_d[tb * TB + p * 128:
                                              tb * TB + (p + 1) * 128, :],
                                    in_=of_b[p % 2][:]),
                        "dou" if p % 2 == 0 else "dou1", 16)
                    oul.append(cnt["dou" if p % 2 == 0 else "dou1"])
            p = GTB - 1
            wt(y, "Y", "cV", cb_cV0 + p * CBV + CBV)
            inc(y.dma_start(out=out_d[tb * TB + p * 128:
                                      tb * TB + (p + 1) * 128, :],
                            in_=of_b[p % 2][:]),
                "dou" if p % 2 == 0 else "dou1", 16)
            oul.append(cnt["dou" if p % 2 == 0 else "dou1"])

        @blk.vector
        def _(v, tb=tb, cb_mT0=cb_mT0, cb_cS0=cb_cS0, xbl=xbl, sul=sul,
              oul=oul, cb_d=cb_d):
            for u in range(GTB):
                wt(v, "V", *sul[u])
                wt(v, "V", "dxb", xbl[u])
                q = "dou" if u % 2 == 0 else "dou1"
                if u >= 2:
                    wt(v, "V", q, oul[u - 2])
                else:
                    wt(v, "V", q, cb_d[u % 2])
                for mc in range(MCN):
                    j = u * MCN + mc
                    jt = j - 2
                    if jt >= 0:
                        wt(v, "V", "mT", cb_mT0 + jt + 1)
                    wt(v, "V", "cS", cb_cS0 + j + 1)
                    wt(v, "V", "cV", cnt["cV"])
                    inc(v.tensor_mul(sgb_b[mc % 2][:],
                                     s_u[u % 4][:, mc * 128:(mc + 1) * 128],
                                     g_sb[j % 2][:]), "cV")
                    wt(v, "V", "mT", cb_mT0 + j + 1)
                    wt(v, "V", "cV", cnt["cV"])
                    inc(v.tensor_add(of_b[u % 2][:, mc * 128:(mc + 1) * 128],
                                     xb_b[:, mc * 128:(mc + 1) * 128],
                                     pbh[6 + mc % 2][:, 0:128]), "cV")
        caps[f"cbV{tb}"] = cnt["cV"]

        @blk.tensor
        def _(t, tb=tb, cb_cV0=cb_cV0):
            for u in range(GTB):
                for mc in range(MCN):
                    wt(t, "T", "cV", cb_cV0 + u * CBV + 2 * mc + 1)
                    j = u * MCN + mc - 2
                    if j >= 0:
                        uj, mj = divmod(j, MCN)
                        wt(t, "T", "cV", cb_cV0 + uj * CBV + 2 * mj + 2)
                    inc(t.transpose(pbh[6 + mc % 2][:, 0:128],
                                    sgb_b[mc % 2][:], id_sb[:]), "mT")

    fin_dou = cnt["dou"]
    fin_dou1 = cnt["dou1"]

    @blk.sync
    def _(y):
        y.wait_ge(sems["dou"], fin_dou)
        y.wait_ge(sems["dou1"], fin_dou1)

    nc._kernel_ctx = ctx
    return nc


def _prep_tables(TC, core_id):
    t0 = core_id * TC
    idx = np.arange(t0, t0 + TC)
    pos = (idx % 8192).astype(np.float32)
    expo = (np.arange(0, 128, 2).astype(np.float32) / np.float32(128.0))
    inv_freq = (np.float32(1.0)
                / np.power(np.float32(10000.0), expo)).astype(np.float32)
    freqs = (pos[:, None] * inv_freq[None, :]).astype(np.float32)
    cos64 = np.cos(freqs).T.astype(np.float32)
    sin64 = np.sin(freqs).T.astype(np.float32)
    cosf = np.concatenate([cos64, cos64], axis=0)
    sinf = np.concatenate([sin64, sin64], axis=0)
    return np.ascontiguousarray(cosf), np.ascontiguousarray(sinf)


def _mask16():
    m = np.zeros((128, 256), dtype=np.float32)
    for t in range(16):
        for hk in range(8):
            for h in range(16):
                m[t * 8 + hk, h * 16 + t] = MASKV
    return m.astype(ml_dtypes.bfloat16)


def _pmat():
    p = np.zeros((128, 128), dtype=np.float32)
    for dp in range(64):
        p[dp + 64, dp] = -1.0
    for dp in range(64, 128):
        p[dp - 64, dp] = 1.0
    return p.astype(ml_dtypes.bfloat16)


def scram_rows(core, TC, quad):
    b, qc = core // quad, core % quad
    RPH = quad * TC // 16
    R = TC // 16
    h = np.arange(16)[:, None, None]
    r = np.arange(R)[None, None, :]
    idx = b * quad * TC + h * RPH + qc * R + r
    return idx.reshape(-1)


def host_prepare(inputs, TC=2048, ncores=NCORE, quad=None):
    if quad is None:
        quad = min(4, ncores)
    Wq = np.asarray(inputs["L0_attn_Wq"])
    Wk = np.asarray(inputs["L0_attn_Wk"])
    Wv = np.asarray(inputs["L0_attn_Wv"])
    Wo = np.asarray(inputs["L0_attn_Wo"])
    R = np.asarray(inputs["L0_attn_rand_matrix"])
    Wr = np.asarray(inputs["L0_ffn_W_router"])
    x = np.asarray(inputs["L0_attn_hidden_states"]).reshape(-1, HID)

    wq = np.ascontiguousarray(Wq.reshape(16, 128, 16, 128).transpose(0, 3, 2, 1)
                              .reshape(16, 128, 2048))
    wk = np.ascontiguousarray(Wk.reshape(8, 128, 16, 128).transpose(0, 3, 2, 1)
                              .reshape(8, 128, 2048))
    wv = np.ascontiguousarray(Wv.reshape(8, 128, 16, 128).transpose(0, 3, 2, 1)
                              .reshape(8, 128, 2048))
    wo = np.ascontiguousarray(Wo.reshape(16, 128, 16, 128).transpose(0, 3, 2, 1)
                              .reshape(16, 128, 2048))
    wr = np.ascontiguousarray(Wr.reshape(16, 128, 16).transpose(1, 0, 2)
                              .reshape(128, 256))
    wup_all = np.concatenate([np.asarray(inputs["L0_ffn_Wsh_up"]),
                              np.asarray(inputs["L0_ffn_Wrt_up"])], axis=0)
    wdn_all = np.concatenate([np.asarray(inputs["L0_ffn_Wsh_down"]),
                              np.asarray(inputs["L0_ffn_Wrt_down"])], axis=0)
    wup = np.ascontiguousarray(wup_all.reshape(NE, 16, 128, 8, 128)
                               .transpose(0, 3, 2, 1, 4).reshape(NE, 8, 128, 2048))
    wdn = np.ascontiguousarray(wdn_all.reshape(NE, 8, 128, 16, 128)
                               .transpose(0, 3, 2, 1, 4).reshape(NE, 16, 128, 1024))
    rt = np.ascontiguousarray(R.T)
    bq = np.ascontiguousarray(np.asarray(inputs["L0_attn_bq"]).reshape(16, 128).T)
    bk = np.ascontiguousarray(np.asarray(inputs["L0_attn_bk"]).reshape(8, 128).T)
    bv = np.ascontiguousarray(np.asarray(inputs["L0_attn_bv"]).reshape(8, 128).T)
    bo = np.ascontiguousarray(np.asarray(inputs["L0_attn_bo"]).reshape(16, 128).T)
    maskc = _mask16()
    identc = np.eye(128, dtype=np.float32).astype(ml_dtypes.bfloat16)
    pmat = _pmat()
    offr = np.zeros((128, 1), dtype=np.float32)
    offr[64:, 0] = 0.25
    offs = np.zeros((128, 1), dtype=np.float32)
    offs[64:, 0] = math.pi / 2.0

    in_maps = []
    for c in range(ncores):
        cosf, sinf = _prep_tables(TC, c)
        in_maps.append(dict(
            x=np.ascontiguousarray(x[c * TC:(c + 1) * TC]),
            xs=np.ascontiguousarray(x[scram_rows(c, TC, quad)]),
            wq=wq, wk=wk, wv=wv, wo=wo, wr=wr, wup=wup, wdn=wdn, rt=rt,
            bq=bq, bk=bk, bv=bv, bo=bo, cosf=cosf, sinf=sinf,
            maskc=maskc, identc=identc, pmat=pmat, offr=offr, offs=offs,
        ))
    return in_maps


def kernel(**inputs):
    TC = 16384 // NCORE
    quad = 4
    in_maps = host_prepare(inputs, TC=TC, quad=quad)
    nc = build_program(TC=TC, quad=quad)
    res = run_bass_kernel_spmd(nc, in_maps, core_ids=list(range(NCORE)))
    out = np.empty((16384, HID), dtype=np.float32)
    for c in range(NCORE):
        out[scram_rows(c, TC, quad)] = res.results[c]["out"]
    return out.reshape(2, 8192, HID)


# version 11
# speedup vs baseline: 1.2385x; 1.2385x over previous
import math
from contextlib import ExitStack

import numpy as np
import ml_dtypes

import concourse.bass as bass
import concourse.mybir as mybir
from concourse.bass_utils import run_bass_kernel_spmd

F32 = mybir.dt.float32
BF16 = mybir.dt.bfloat16
AF = mybir.ActivationFunctionType
ALU = mybir.AluOpType
AX = mybir.AxisListType

NCORE = 8
HID = 2048
KCN = 16
H = 16
HKV = 8
NHEADS = H + HKV
DFF = 1024
DCN = 8
MCN = 16
NRT = 16
NSH = 2
NE = NSH + NRT
EG = 3
NEG = NE // EG
TWO_PI = 6.283185307179586
INV_2PI = 1.0 / TWO_PI
MAGIC = 12582912.0
MASKV = 2.0 / 128.0

SEM_NAMES = (["mT", "cV", "cS", "cP", "dCY", "dCG", "dsp", "dxt", "dgr", "dst",
              "dxb", "dou", "dou1", "dat", "dcc", "dwi0", "dwi1"]
             + [f"dx{i}" for i in range(2)] + [f"dg{i}" for i in range(2)]
             + [f"dn{i}" for i in range(2)] + [f"dw{i}" for i in range(3)]
             + [f"du{i}" for i in range(2)] + [f"dd{i}" for i in range(6)])


def build_program(TC=2048, quad=4):
    assert TC % 128 == 0
    NG = TC // 128
    ST = min(512, TC)
    NST = TC // ST
    STG = ST // 16
    TB = min(1024, TC)
    NTB = TC // TB
    NCH = max(1, TB // 512)
    CH = TB // NCH
    GTB = TB // 128
    SGC = ST // 16
    HPB = 16 // quad
    NTC2 = TC // 512
    RPH = quad * TC // 16
    HIC = max(1, 512 // RPH)

    nc = bass.Bass("TRN2", target_bir_lowering=False)

    din = lambda n, s, d=F32: nc.dram_tensor(n, s, d, kind="ExternalInput").ap()
    x_d = din("x", [TC, HID])
    xs_d = din("xs", [TC, HID])
    wq_d = din("wq", [16, 128, 2048])
    wk_d = din("wk", [8, 128, 2048])
    wv_d = din("wv", [8, 128, 2048])
    wo_d = din("wo", [16, 128, 2048])
    wr_d = din("wr", [128, 256])
    wup_d = din("wup", [NE, DCN, 128, 2048])
    wdn_d = din("wdn", [NE, MCN, 128, 1024])
    rt_d = din("rt", [128, 64])
    bq_d = din("bq", [128, 16])
    bk_d = din("bk", [128, 8])
    bv_d = din("bv", [128, 8])
    bo_d = din("bo", [128, 16])
    cos_d = din("cosf", [128, TC])
    sin_d = din("sinf", [128, TC])
    mask_d = din("maskc", [128, 256], BF16)
    id_d = din("identc", [128, 128], BF16)
    pm_d = din("pmat", [128, 128], BF16)
    offr_d = din("offr", [128, 1])
    offs_d = din("offs", [128, 1])

    out_d = nc.dram_tensor("out", [TC, HID], F32, kind="ExternalOutput").ap()
    xn_dram = nc.dram_tensor("xn_scr", [128, KCN * TC], BF16, kind="Internal").ap()
    xns_dram = nc.dram_tensor("xns_scr", [128, KCN * TC], BF16,
                              kind="Internal").ap()
    s_dram = nc.dram_tensor("s_scr", [128, MCN * TC], BF16, kind="Internal").ap()
    satt_dram = nc.dram_tensor("satt_scr", [128, 16 * TC], BF16,
                               kind="Internal").ap()

    ctx = ExitStack()
    E = ctx.enter_context
    sbuf = lambda n, s, d: E(nc.sbuf_tensor(n, s, d))

    rt_sb = sbuf("rt_sb", [128, 64], BF16)
    wr_sb = sbuf("wr_sb", [128, 256], BF16)
    mask_sb = sbuf("mask_sb", [128, 256], BF16)
    id_sb = sbuf("id_sb", [128, 128], BF16)
    pm_sb = sbuf("pm_sb", [128, 128], BF16)
    bq_sb = sbuf("bq_sb", [128, 16], F32)
    bk_sb = sbuf("bk_sb", [128, 8], F32)
    bv_sb = sbuf("bv_sb", [128, 8], F32)
    bo_sb = sbuf("bo_sb", [128, 16], F32)
    cos_sb = sbuf("cos_sb", [128, TC], BF16)
    sin_sb = sbuf("sin_sb", [128, TC], BF16)
    offr_sb = sbuf("offr_sb", [128, 1], F32)
    offs_sb = sbuf("offs_sb", [128, 1], F32)
    ones128 = sbuf("ones128", [128, 128], BF16)
    eps_sb = sbuf("eps_sb", [128, 1], F32)
    eps12_sb = sbuf("eps12_sb", [128, 1], F32)

    ph1_ctx = ExitStack()
    E1 = ph1_ctx.enter_context
    sb1 = lambda n, s, d: E1(nc.sbuf_tensor(n, s, d))
    x_tm = [sb1(f"x_tm{i}", [128, HID], F32) for i in range(2)]
    xsq = sb1("xsq", [128, HID], F32)
    st_s1 = sb1("st_s1", [128, 1], F32)
    st_negm = sb1("st_negm", [128, 1], F32)
    st_ss = sb1("st_ss", [128, 1], F32)
    st_var = sb1("st_var", [128, 1], F32)
    st_m2 = sb1("st_m2", [128, 1], F32)
    st_std = sb1("st_std", [128, 1], F32)
    st_rstd = sb1("st_rstd", [128, 1], F32)
    xn_tm = sb1("xn_tm", [128, HID], BF16)
    xn_stage = [sb1(f"xn_stage{i}", [128, HID], BF16) for i in range(2)]

    pb = [E(nc.psum_tensor(f"pb{i}", [128, 512], F32)) for i in range(8)]
    pbh = [p.bitcast(BF16) for p in pb]

    sems, cnt = {}, {}
    for s in SEM_NAMES:
        sems[s] = E(nc.semaphore(s))
        cnt[s] = 0
    seen = {e: {s: 0 for s in sems} for e in "TVSGY"}

    def wt(eng, ekey, s, c):
        if c > seen[ekey][s]:
            eng.wait_ge(sems[s], c)
            seen[ekey][s] = c

    def inc(inst, s, k=1):
        inst.then_inc(sems[s], k)
        cnt[s] += k

    blk = E(nc.Block())

    @blk.sync
    def _(y):
        for dst, src in [(mask_sb, mask_d), (id_sb, id_d), (pm_sb, pm_d),
                         (bq_sb, bq_d), (bk_sb, bk_d), (bv_sb, bv_d),
                         (bo_sb, bo_d), (offr_sb, offr_d), (offs_sb, offs_d)]:
            inc(y.dma_start(out=dst[:], in_=src), "dCY", 16)

    @blk.gpsimd
    def _(g):
        for dst, src in [(rt_sb, rt_d), (wr_sb, wr_d), (cos_sb, cos_d),
                         (sin_sb, sin_d)]:
            inc(g.dma_start(out=dst[:], in_=src), "dCG", 16)

    cCY, cCG = cnt["dCY"], cnt["dCG"]

    @blk.vector
    def _(v):
        wt(v, "V", "dCY", cCY)
        wt(v, "V", "dCG", cCG)
        inc(v.memset(ones128[:], 1.0), "cV")
        wt(v, "V", "cV", cnt["cV"])
        inc(v.memset(eps_sb[:], 1e-5), "cV")
        wt(v, "V", "cV", cnt["cV"])
        inc(v.memset(eps12_sb[:], 1e-12), "cV")

    PV = 7
    PS = 2 + 4

    def emit_ln(src_d, dst_dram, base):
        ph1_cV0 = cnt["cV"]
        ph1_cS0 = cnt["cS"]
        ph1_mT0 = cnt["mT"]
        xld = lambda g: 16 * (base * (NG // 2) + g // 2 + 1)
        gld = lambda g: 16 * (base * (NG // 2) + g // 2 + 1)

        @blk.sync
        def _(y):
            def spill(g):
                wt(y, "Y", "cS", ph1_cS0 + g * PS + PS)
                ap = bass.AP(dst_dram.tensor, g * 128,
                             [[KCN * TC, 128], [TC, KCN], [1, 128]])
                inc(y.dma_start(out=ap, in_=xn_stage[g % 2][:]),
                    f"dg{g % 2}", 16)

            for g in range(NG):
                if g >= 2 or base:
                    wt(y, "Y", "cV", ph1_cV0 + (g - 2) * PV + 7)
                    wt(y, "Y", "cS", ph1_cS0 + (g - 2) * PS + 1)
                inc(y.dma_start(out=x_tm[g % 2][:],
                                in_=src_d[g * 128:(g + 1) * 128, :]),
                    f"dx{g % 2}", 16)
                if g >= 1:
                    spill(g - 1)
            spill(NG - 1)

        @blk.vector
        def _(v):
            for g in range(NG):
                wt(v, "V", f"dx{g % 2}", xld(g))
                xg = x_tm[g % 2]
                wt(v, "V", "cV", cnt["cV"])
                inc(v.reduce_sum(st_s1[:], xg[:], axis=AX.X), "cV")
                wt(v, "V", "cV", cnt["cV"])
                inc(v.tensor_scalar(st_negm[:], st_s1[:], -1.0 / HID, None,
                                    ALU.mult), "cV")
                wt(v, "V", "cS", ph1_cS0 + g * PS + 1)
                wt(v, "V", "cV", cnt["cV"])
                inc(v.tensor_scalar(st_var[:], st_ss[:], 1.0 / HID, None,
                                    ALU.mult), "cV")
                wt(v, "V", "cV", cnt["cV"])
                inc(v.tensor_mul(st_m2[:], st_negm[:], st_negm[:]), "cV")
                wt(v, "V", "cV", cnt["cV"])
                inc(v.tensor_sub(st_var[:], st_var[:], st_m2[:]), "cV")
                wt(v, "V", "cS", ph1_cS0 + g * PS + 2)
                wt(v, "V", "cV", cnt["cV"])
                inc(v.reciprocal(st_rstd[:], st_std[:]), "cV")
                if g >= 1 or base:
                    wt(v, "V", "mT", ph1_mT0 + (g - 1) * 16 + 16)
                wt(v, "V", "cV", cnt["cV"])
                inc(v.tensor_scalar(xn_tm[:], xg[:], st_negm[:], st_rstd[:],
                                    ALU.add, ALU.mult), "cV")


        @blk.scalar
        def _(s):
            for g in range(NG):
                wt(s, "S", f"dx{g % 2}", xld(g))
                if g >= 1 or base:
                    wt(s, "S", "cV", ph1_cV0 + (g - 1) * PV + 3)
                wt(s, "S", "cS", cnt["cS"])
                inc(s.activation(out=xsq[:], in_=x_tm[g % 2][:],
                                 func=AF.Square, accum_out=st_ss[:]), "cS")
                wt(s, "S", "cV", ph1_cV0 + g * PV + 5)
                wt(s, "S", "cS", cnt["cS"])
                inc(s.activation(out=st_std[:], in_=st_var[:], func=AF.Sqrt,
                                 bias=eps_sb[:], scale=1.0), "cS")
                if g >= 2 or base:
                    wt(s, "S", f"dg{g % 2}", gld(g - 2))
                for q4 in range(4):
                    wt(s, "S", "mT", ph1_mT0 + g * 16 + q4 * 4 + 4)
                    wt(s, "S", "cS", cnt["cS"])
                    inc(s.activation(out=xn_stage[g % 2][:, q4 * 512:
                                                         (q4 + 1) * 512],
                                     in_=pbh[2 + q4 % 2][:, 0:512],
                                     func=AF.Copy, bias=0.0, scale=1.0), "cS")

        @blk.tensor
        def _(t):
            wt(t, "T", "dCY", cCY)
            for g in range(NG):
                wt(t, "T", "cV", ph1_cV0 + g * PV + 7)
                for c in range(KCN):
                    q4, sl = divmod(c, 4)
                    done4 = g * 4 + q4 - 2
                    if (done4 >= 0 or base) and sl == 0:
                        wt(t, "T", "cS",
                           ph1_cS0 + (done4 // 4) * PS + 2 + (done4 % 4) + 1)
                    inc(t.transpose(pbh[2 + q4 % 2][:, sl * 128:
                                                    (sl + 1) * 128],
                                    xn_tm[:, c * 128:(c + 1) * 128],
                                    id_sb[:]), "mT")

    emit_ln(x_d, xn_dram, 0)
    emit_ln(xs_d, xns_dram, 1)

    ph1_ctx.close()
    ph2_ctx = ExitStack()
    E2 = ph2_ctx.enter_context
    sb2 = lambda n, s, d: E2(nc.sbuf_tensor(n, s, d))
    xn_st = [sb2("xn_st0", [128, KCN * ST], BF16)] * 2
    wbuf = [sb2(f"wbuf{i}", [128, 2048], BF16) for i in range(3)]
    q_f = [sb2(f"q_f{i}", [128, H * ST], BF16) for i in range(2)]
    k_f = [sb2(f"k_f{i}", [128, HKV * ST], BF16) for i in range(2)]
    v_gf = [sb2(f"v_gf{i}", [128, HKV * ST], BF16) for i in range(2)]
    pq_sb = [sb2(f"pq_sb{i}", [128, ST], BF16) for i in range(4)]
    fq = sb2("fq", [128, STG * 256], BF16)
    fk = sb2("fk", [128, STG * 128], BF16)
    qr_b = [sb2(f"qr_b{i}", [128, ST], BF16) for i in range(4)]
    sq_b = [sb2(f"sq_b{i}", [128, ST], BF16) for i in range(4)]
    un_b = [sb2(f"un_b{i}", [128, ST], BF16) for i in range(4)]
    nsq_b = [sb2(f"nsq_b{i}", [128, ST], F32) for i in range(4)]
    invn_b = nsq_b
    rrA = [sb2(f"rrA{i}", [128, ST], F32) for i in range(4)]
    parg_b = rrA
    v16_b = [sb2(f"v16_b{i}", [128, 128], BF16) for i in range(2)]
    am_b = [sb2(f"am_b{i}", [128, 256], BF16) for i in range(2)]
    attn_st = sb2("attn_st", [128, H * ST], BF16)
    s_st = sb2("s_st", [128, MCN * 512], BF16)
    woin_b = [sb2("woin_b0", [128, 16 * 512], BF16), attn_st]

    def _barrier(bar):
        for dec, ekey in [(blk.tensor, "T"), (blk.vector, "V"),
                          (blk.scalar, "S"), (blk.gpsimd, "G"),
                          (blk.sync, "Y")]:
            def _f(eng, ekey=ekey, bar=bar):
                own = {"T": "mT", "V": "cV", "S": "cS"}.get(ekey)
                for s, c in bar.items():
                    if s != own and c > 0:
                        wt(eng, ekey, s, c)
            dec(_f)

    _barrier(dict(cnt))

    NOC = H + HKV + HKV
    qkvsrc = ([(wq_d, oc) for oc in range(16)] + [(wk_d, oc) for oc in range(8)]
              + [(wv_d, oc) for oc in range(8)])
    caps = {}
    NV = 5
    PP = 5
    cv16 = lambda g: 2 * g + 1
    camk = lambda g: 2 * g + 2

    def ev_at(st, j):
        if st == 0:
            return caps["ev0"] + j + 1
        C = caps[st - 1]
        return C["S0"] + C["Spos"][("evq", j)]

    def qkvT_at(st, j):
        if st == 0:
            return caps["qkv0"] + j + 1
        C = caps[st - 1]
        return C["T0"] + C["Tpos"][("qkv", j)]

    @blk.sync
    def _(y):
        ap = bass.AP(xn_dram.tensor, 0, [[KCN * TC, 128], [TC, KCN], [1, ST]])
        inc(y.dma_start(out=xn_st[0][:], in_=ap), "dn0", 16)
    xn_ld0 = ("dn0", cnt["dn0"])

    caps["qkv0"] = cnt["mT"]
    caps["ev0"] = cnt["cS"]
    wld0 = []

    @blk.gpsimd
    def _(g):
        for j in range(NOC):
            if j >= 3:
                wt(g, "G", "mT", qkvT_at(0, j - 3))
            sname = f"dw{j % 3}"
            inc(g.dma_start(out=wbuf[j % 3][:],
                            in_=qkvsrc[j][0][qkvsrc[j][1]]), sname, 16)
            wld0.append((sname, cnt[sname]))

    @blk.tensor
    def _(t):
        wt(t, "T", *xn_ld0)
        for j in range(NOC):
            wt(t, "T", *wld0[j])
            if j >= 2:
                wt(t, "T", "cS", ev_at(0, j - 2))
            else:
                wt(t, "T", "cS", caps["ev0"])
            for kc in range(KCN):
                ins = t.matmul(pb[j % 2][:, 0:ST],
                               wbuf[j % 3][:, kc * 128:(kc + 1) * 128],
                               xn_st[0][:, kc * ST:(kc + 1) * ST],
                               start=(kc == 0), stop=(kc == KCN - 1))
            inc(ins, "mT")

    def emit_qkv_evac(s, st, j):
        if j < 16:
            dst = q_f[st % 2][:, j * ST:(j + 1) * ST]
            bias = bq_sb[:, j:j + 1]
        elif j < 24:
            hk = j - 16
            dst = k_f[st % 2][:, hk * ST:(hk + 1) * ST]
            bias = bk_sb[:, hk:hk + 1]
        else:
            hk = j - 24
            dst = bass.AP(v_gf[st % 2], hk,
                          [[HKV * ST, 128], [128, STG], [8, 16]])
            bias = bv_sb[:, hk:hk + 1]
        inc(s.activation(out=dst, in_=pb[j % 2][:, 0:ST], func=AF.Identity,
                         bias=bias, scale=1.0), "cS")

    @blk.scalar
    def _(s):
        for j in range(NOC):
            wt(s, "S", "mT", qkvT_at(0, j))
            wt(s, "S", "cS", cnt["cS"])
            emit_qkv_evac(s, 0, j)

    for st in range(NST):
        nxt = st + 1 < NST
        C = caps.setdefault(st, {})
        C["T0"] = T0 = cnt["mT"]
        C["S0"] = S0 = cnt["cS"]
        C["G0"] = G0 = cnt["cV"]

        t_ops = []
        for hh in range(NHEADS + 3):
            if hh >= 3:
                t_ops.append(("phi", hh - 3))
            if hh < NHEADS:
                t_ops.append(("nsq", hh))
            if nxt and 2 <= hh < NHEADS + 2:
                t_ops.append(("qkv", hh - 2))
            if hh < NHEADS:
                t_ops.append(("pq", hh))
        if nxt:
            for j in range(NHEADS, NOC):
                t_ops.append(("qkv", j))
        Tpos = {op: i + 1 for i, op in enumerate(t_ops)}
        C["Tpos"] = Tpos
        co_mT0 = T0 + len(t_ops)

        s_ops = []
        for g in range(NHEADS + 4):
            if g < NHEADS:
                s_ops.append(("pqe", g))
            if 1 <= g <= NHEADS:
                s_ops.append(("rsq", g - 1))
            if 3 <= g < NHEADS + 3:
                s_ops.append(("rr1", g - 3))
            if nxt and 2 <= g < NHEADS + 2:
                s_ops.append(("evq", g - 2))
            if 4 <= g < NHEADS + 4:
                s_ops.append(("sin", g - 4))
        if nxt:
            for j in range(NHEADS, NOC):
                s_ops.append(("evq", j))
        Spos = {op: i + 1 for i, op in enumerate(s_ops)}
        C["Spos"] = Spos

        v_ops = []
        for sl in range(NHEADS + 2):
            if sl < NHEADS:
                v_ops.append(("vtmp", sl))
                v_ops.append(("vadd", sl))
                v_ops.append(("vrcp", sl))
                v_ops.append(("vunit", sl))
            if sl >= 2:
                v_ops.append(("vrr2", sl - 2))
                v_ops.append(("vrr3", sl - 2))
        Vpos = {op: i + 1 for i, op in enumerate(v_ops)}
        C["Vpos"] = Vpos

        C["P0"] = P0 = cnt["cP"]
        C["glue_cV_end"] = G0 + len(v_ops)
        C["glue_cP_end"] = P0 + 2 * NHEADS
        coV0 = G0 + len(v_ops)
        coS0 = S0 + len(s_ops)

        if nxt:
            @blk.sync
            def _(y, st=st):
                wt(y, "Y", "mT", qkvT_at(st, NOC - 1))
                ap = bass.AP(xn_dram.tensor, (st + 1) * ST,
                             [[KCN * TC, 128], [TC, KCN], [1, ST]])
                inc(y.dma_start(out=xn_st[(st + 1) % 2][:], in_=ap),
                    f"dn{(st + 1) % 2}", 16)
            C["xn_ld"] = (f"dn{(st + 1) % 2}", cnt[f"dn{(st + 1) % 2}"])


        wldn = []
        C["wld"] = wldn

        @blk.gpsimd
        def _(g, st=st, nxt=nxt, T0=T0, S0=S0, G0=G0, P0=P0, Tpos=Tpos,
              Spos=Spos, Vpos=Vpos, wldn=wldn):
            c_ = cos_sb[:, st * ST:(st + 1) * ST]

            def dw(j):
                if j >= 3:
                    wt(g, "G", "mT", qkvT_at(st + 1, j - 3))
                else:
                    wt(g, "G", "mT", qkvT_at(st, NOC - 1))
                sname = f"dw{j % 3}"
                inc(g.dma_start(out=wbuf[j % 3][:],
                                in_=qkvsrc[j][0][qkvsrc[j][1]]), sname, 16)
                wldn.append((sname, cnt[sname]))

            def pops(k):
                isq = k < H
                srcT = (q_f if isq else k_f)[st % 2]
                base = (k if isq else k - H) * ST
                src_ = srcT[:, base:base + ST]
                wt(g, "G", "cS", ev_at(st, k))
                if k >= 4:
                    wt(g, "G", "cV", G0 + Vpos[("vadd", k - 4)])
                elif st >= 1:
                    wt(g, "G", "cV", caps[st - 1]["glue_cV_end"])
                wt(g, "G", "cP", cnt["cP"])
                inc(g.tensor_mul(sq_b[k % 4][:], src_, src_), "cP")
                if k >= 4:
                    wt(g, "G", "cV", G0 + Vpos[("vunit", k - 4)])
                wt(g, "G", "cP", cnt["cP"])
                inc(g.tensor_mul(qr_b[k % 4][:], src_, c_), "cP")

            if nxt:
                for j in range(3):
                    dw(j)
            pops(0)
            pops(1)
            for bb in range(NHEADS):
                if bb + 2 < NHEADS:
                    pops(bb + 2)
                if nxt and 3 <= bb + 2 < NOC:
                    dw(bb + 2)
            if nxt:
                for j in range(NHEADS + 2, NOC):
                    dw(j)

        @blk.scalar
        def _(s, st=st, nxt=nxt, T0=T0, S0=S0, G0=G0, Tpos=Tpos, Spos=Spos,
              Vpos=Vpos, s_ops=s_ops):
            if st >= 1:
                wt(s, "S", "mT", caps[st - 1]["core_mT_end"])
                wt(s, "S", "cV", caps[st - 1]["glue_cV_end"])
                wt(s, "S", "cP", caps[st - 1]["glue_cP_end"])
            for op in s_ops:
                kind, k = op
                if kind == "pqe":
                    wt(s, "S", "mT", T0 + Tpos[("pq", k)])
                    if k >= 4:
                        wt(s, "S", "cV", G0 + Vpos[("vtmp", k - 4)])
                    wt(s, "S", "cS", cnt["cS"])
                    inc(s.activation(out=pq_sb[k % 4][:],
                                     in_=pb[4 + k % 2][:, 0:ST],
                                     func=AF.Copy, bias=0.0, scale=1.0), "cS")
                elif kind == "rsq":
                    wt(s, "S", "mT", T0 + Tpos[("nsq", k)])
                    if k >= 4:
                        wt(s, "S", "cV", G0 + Vpos[("vunit", k - 4)])
                    wt(s, "S", "cS", cnt["cS"])
                    inc(s.activation(out=nsq_b[k % 4][:],
                                     in_=pb[2 + k % 2][:, 0:ST],
                                     func=AF.Sqrt, bias=eps12_sb[:],
                                     scale=1.0), "cS")
                elif kind == "rr1":
                    wt(s, "S", "mT", T0 + Tpos[("phi", k)])
                    if k >= 4:
                        wt(s, "S", "cV", G0 + Vpos[("vrr3", k - 4)])
                    wt(s, "S", "cS", cnt["cS"])
                    inc(s.activation(out=rrA[k % 4][:],
                                     in_=pb[6 + k % 2][:, 0:ST],
                                     func=AF.Identity, bias=offr_sb[:],
                                     scale=INV_2PI), "cS")
                elif kind == "sin":
                    wt(s, "S", "cV", G0 + Vpos[("vrr3", k)])
                    wt(s, "S", "cS", cnt["cS"])
                    if k < H:
                        dst = bass.AP(fq, k * 16,
                                      [[STG * 256, 128], [256, STG], [1, 16]])
                    else:
                        dst = bass.AP(fk, k - H,
                                      [[STG * 128, 128], [128, STG], [8, 16]])
                    inc(s.activation(out=dst, in_=parg_b[k % 4][:],
                                     func=AF.Sin, bias=offs_sb[:], scale=1.0),
                        "cS")
                else:
                    wt(s, "S", "mT", T0 + Tpos[("qkv", k)])
                    wt(s, "S", "cS", cnt["cS"])
                    emit_qkv_evac(s, st + 1, k)

        @blk.vector
        def _(v, st=st, T0=T0, S0=S0, P0=P0, Tpos=Tpos, Spos=Spos,
              v_ops=v_ops):
            s_ = sin_sb[:, st * ST:(st + 1) * ST]
            for op in v_ops:
                kind, k = op
                if kind == "vtmp":
                    wt(v, "V", "cS", S0 + Spos[("pqe", k)])
                    wt(v, "V", "mT", T0 + Tpos[("nsq", k)])
                    wt(v, "V", "cV", cnt["cV"])
                    inc(v.tensor_mul(sq_b[k % 4][:], pq_sb[k % 4][:], s_),
                        "cV")
                elif kind == "vadd":
                    wt(v, "V", "cP", P0 + 2 * k + 2)
                    wt(v, "V", "cV", cnt["cV"])
                    inc(v.tensor_add(qr_b[k % 4][:], qr_b[k % 4][:],
                                     sq_b[k % 4][:]), "cV")
                elif kind == "vrcp":
                    wt(v, "V", "cS", S0 + Spos[("rsq", k)])
                    wt(v, "V", "cV", cnt["cV"])
                    inc(v.reciprocal(invn_b[k % 4][:], nsq_b[k % 4][:]), "cV")
                elif kind == "vunit":
                    if k >= 4:
                        wt(v, "V", "mT", T0 + Tpos[("phi", k - 4)])
                    wt(v, "V", "cV", cnt["cV"])
                    inc(v.tensor_mul(un_b[k % 4][:], qr_b[k % 4][:],
                                     invn_b[k % 4][:]), "cV")
                elif kind == "vrr2":
                    wt(v, "V", "cS", S0 + Spos[("rr1", k)])
                    wt(v, "V", "cV", cnt["cV"])
                    inc(v.tensor_scalar(rrA[k % 4][:], rrA[k % 4][:], MAGIC,
                                        MAGIC, ALU.add, ALU.subtract), "cV")
                else:
                    if k >= 4:
                        wt(v, "V", "cS", S0 + Spos[("sin", k - 4)])
                    wt(v, "V", "cV", cnt["cV"])
                    inc(v.scalar_tensor_tensor(parg_b[k % 4][:], rrA[k % 4][:],
                                               -TWO_PI, pb[6 + k % 2][:, 0:ST],
                                               ALU.mult, ALU.add), "cV")

        @blk.tensor
        def _(t, st=st, nxt=nxt, T0=T0, S0=S0, G0=G0, P0=P0, Tpos=Tpos,
              Spos=Spos, Vpos=Vpos, t_ops=t_ops, co_mT0=co_mT0,
              coV0=coV0, C=C):
            wt(t, "T", "dCG", cCG)
            for op in t_ops:
                kind, k = op
                if kind == "pq":
                    isq = k < H
                    srcT = (q_f if isq else k_f)[st % 2]
                    base = (k if isq else k - H) * ST
                    wt(t, "T", "cS", ev_at(st, k))
                    if k >= 2:
                        wt(t, "T", "cS", S0 + Spos[("pqe", k - 2)])
                    elif st >= 1:
                        wt(t, "T", "cV", caps[st - 1]["coreV_end"])
                    inc(t.matmul(pb[4 + k % 2][:, 0:ST], pm_sb[:],
                                 srcT[:, base:base + ST], start=True,
                                 stop=True), "mT")
                elif kind == "nsq":
                    wt(t, "T", "cP", P0 + 2 * k + 1)
                    if k >= 2:
                        wt(t, "T", "cS", S0 + Spos[("rsq", k - 2)])
                    elif st >= 1:
                        wt(t, "T", "cV", caps[st - 1]["coreV_end"])
                    inc(t.matmul(pb[2 + k % 2][:, 0:ST], ones128[:],
                                 sq_b[k % 4][:], start=True, stop=True), "mT")
                elif kind == "phi":
                    wt(t, "T", "cV", G0 + Vpos[("vunit", k)])
                    if k >= 2:
                        wt(t, "T", "cV", G0 + Vpos[("vrr3", k - 2)])
                    elif st >= 1:
                        wt(t, "T", "cV", caps[st - 1]["coreV_end"])
                    t.matmul(pb[6 + k % 2][0:64, 0:ST], rt_sb[:],
                             un_b[k % 4][:], start=True, stop=True)
                    inc(t.matmul(pb[6 + k % 2][64:128, 0:ST], rt_sb[:],
                                 un_b[k % 4][:], start=True, stop=True), "mT")
                else:
                    j = k
                    wt(t, "T", *C["wld"][j])
                    wt(t, "T", *C["xn_ld"])
                    if j >= 2:
                        wt(t, "T", "cS", ev_at(st + 1, j - 2))
                    else:
                        wt(t, "T", "cS", ev_at(st, NOC - 1))
                    for kc in range(KCN):
                        ins = t.matmul(pb[j % 2][:, 0:ST],
                                       wbuf[j % 3][:, kc * 128:(kc + 1) * 128],
                                       xn_st[(st + 1) % 2][:, kc * ST:
                                                           (kc + 1) * ST],
                                       start=(kc == 0), stop=(kc == KCN - 1))
                    inc(ins, "mT")
            wt(t, "T", "cS", S0 + Spos[("sin", NHEADS - 1)])
            for gg in range(STG):
                if gg >= 2:
                    wt(t, "T", "cV", coV0 + cv16(gg - 2))
                else:
                    wt(t, "T", "cV", C["glue_cV_end"])
                inc(t.transpose(pbh[2 + gg % 2][:, 0:128],
                                v_gf[st % 2][:, gg * 128:(gg + 1) * 128],
                                id_sb[:]), "mT")
                if gg >= 2:
                    wt(t, "T", "cV", coV0 + camk(gg - 2))
                inc(t.matmul(pb[4 + gg % 2][:, 0:256],
                             fk[:, gg * 128:(gg + 1) * 128],
                             fq[:, gg * 256:(gg + 1) * 256],
                             start=True, stop=True), "mT")
                if gg >= 1:
                    p = gg - 1
                    wt(t, "T", "cV", coV0 + camk(p))
                    if p >= 2:
                        wt(t, "T", "cS", coS0 + (p - 2) + 1)
                    inc(t.matmul(pb[6 + p % 2][:, 0:256], v16_b[p % 2][:],
                                 am_b[p % 2][:], start=True, stop=True), "mT")
            p = STG - 1
            wt(t, "T", "cV", coV0 + camk(p))
            if p >= 2:
                wt(t, "T", "cS", coS0 + (p - 2) + 1)
            inc(t.matmul(pb[6 + p % 2][:, 0:256], v16_b[p % 2][:],
                         am_b[p % 2][:], start=True, stop=True), "mT")
        C["core_mT_end"] = cnt["mT"]

        @blk.vector
        def _(v, st=st, co_mT0=co_mT0, C=C):
            for gg in range(STG):
                wt(v, "V", "mT", co_mT0 + gg * 3 + 1)
                wt(v, "V", "cV", cnt["cV"])
                inc(v.tensor_copy(v16_b[gg % 2][:], pbh[2 + gg % 2][:, 0:128]),
                    "cV")
                wt(v, "V", "mT", co_mT0 + gg * 3 + 2)
                wt(v, "V", "cV", cnt["cV"])
                inc(v.tensor_mul(am_b[gg % 2][:], pb[4 + gg % 2][:, 0:256],
                                 mask_sb[:]), "cV")
        C["coreV_end"] = cnt["cV"]

        @blk.scalar
        def _(s, st=st, co_mT0=co_mT0, C=C, _at=cnt["dat"]):
            if st >= 1:
                wt(s, "S", "dat", _at)

            def _evdst(p):
                return bass.AP(attn_st, p, [[H * ST, 128], [ST, H], [SGC, 16]])

            for p in range(STG):
                if p < STG - 1:
                    wt(s, "S", "mT", co_mT0 + (p + 1) * 3 + 3)
                else:
                    wt(s, "S", "mT", C["core_mT_end"])
                wt(s, "S", "cS", cnt["cS"])
                inc(s.activation(out=_evdst(p), in_=pb[6 + p % 2][:, 0:256],
                                 func=AF.Copy, bias=0.0, scale=1.0), "cS")
        C["coreS_end"] = cnt["cS"]

        @blk.sync
        def _(y, st=st, _ce=C["coreS_end"]):
            wt(y, "Y", "cS", _ce)
            for h in range(16):
                dst = bass.AP(satt_dram.tensor, h * TC + st * SGC,
                              [[16 * TC, 128], [TC // 16, 16], [1, SGC]])
                src = bass.AP(attn_st, h * ST,
                              [[H * ST, 128], [SGC, 16], [1, SGC]])
                inc(y.dma_start(out=dst, in_=src), "dat", 16)

    att_done = cnt["dat"]
    HIC2 = 512 // (TC // 16)

    for tc2 in range(NTC2):
        wo_mT0 = cnt["mT"]
        wo_cS0 = cnt["cS"]
        wld2 = []
        wil = []

        @blk.sync
        def _(y, tc2=tc2, wo_mT0=wo_mT0):
            wt(y, "Y", "dat", att_done)
            if tc2 >= 2:
                wt(y, "Y", "mT", caps[f"wo{tc2 - 2}"])
            for half in range(2):
                sname = f"dwi{half}"
                for j in range(half * 8, half * 8 + 8):
                    dims = [[16 * TC, 128]]
                    if HIC2 > 1:
                        dims.append([TC, HIC2])
                    dims.append([1, TC // 16])
                    src = bass.AP(satt_dram.tensor,
                                  tc2 * HIC2 * TC + j * (TC // 16), dims)
                    inc(y.dma_start(
                        out=woin_b[tc2 % 2][:, j * 512:(j + 1) * 512],
                        in_=src), sname, 16)
                    wil.append((sname, cnt[sname]))

        @blk.gpsimd
        def _(g, tc2=tc2, wo_mT0=wo_mT0):
            for oc in range(16):
                if oc >= 3:
                    wt(g, "G", "mT", wo_mT0 + (oc - 3) + 1)
                elif tc2 >= 1:
                    wt(g, "G", "mT", caps[f"wo{tc2 - 1}"])
                else:
                    wt(g, "G", "mT", qkvT_at(NST - 1, NOC - 1))
                sname = f"dw{oc % 3}"
                inc(g.dma_start(out=wbuf[oc % 3][:], in_=wo_d[oc]), sname, 16)
                wld2.append((sname, cnt[sname]))

        @blk.tensor
        def _(t, tc2=tc2, wo_cS0=wo_cS0, wld2=wld2, wil=wil):
            wt(t, "T", *wil[7])
            for oc in range(16):
                wt(t, "T", *wld2[oc])
                if oc >= 2:
                    wt(t, "T", "cS", wo_cS0 + (oc - 2) + 1)
                else:
                    wt(t, "T", "cS", wo_cS0)
                for kc in range(KCN):
                    if oc == 0 and kc == 8:
                        wt(t, "T", *wil[-1])
                    ins = t.matmul(pb[oc % 2][:, 0:512],
                                   wbuf[oc % 3][:, kc * 128:(kc + 1) * 128],
                                   woin_b[tc2 % 2][:, kc * 512:(kc + 1) * 512],
                                   start=(kc == 0), stop=(kc == KCN - 1))
                inc(ins, "mT")
        caps[f"wo{tc2}"] = cnt["mT"]

        @blk.scalar
        def _(s, tc2=tc2, wo_mT0=wo_mT0, _sp=cnt["dsp"]):
            if tc2 >= 1:
                wt(s, "S", "dsp", _sp)
            for oc in range(16):
                wt(s, "S", "mT", wo_mT0 + oc + 1)
                wt(s, "S", "cS", cnt["cS"])
                inc(s.activation(out=s_st[:, oc * 512:(oc + 1) * 512],
                                 in_=pb[oc % 2][:, 0:512], func=AF.Identity,
                                 bias=bo_sb[:, oc:oc + 1], scale=1.0), "cS")

        s_cS_end = cnt["cS"]

        @blk.sync
        def _(y, tc2=tc2, s_cS_end=s_cS_end):
            wt(y, "Y", "cS", s_cS_end)
            ap = bass.AP(s_dram.tensor, tc2 * 512,
                         [[MCN * TC, 128], [TC, MCN], [1, 512]])
            inc(y.dma_start(out=ap, in_=s_st[:]), "dsp", 16)

    ph2_ctx.close()
    ph3_ctx = ExitStack()
    E3 = ph3_ctx.enter_context
    sb3 = lambda n, s, d: E3(nc.sbuf_tensor(n, s, d))
    xn_tb = sb3("xn_tb", [128, KCN * TB], BF16)
    ones1 = sb3("ones1", [65, 128], BF16)
    acc_tb = sb3("acc_tb", [128, MCN * TB], F32)
    mids = sb3("mids", [128, EG * DCN * TB], BF16)
    wup_b = [sb3(f"wup_b{i}", [128, 2048], BF16) for i in range(2)]
    wdn_b = [sb3(f"wdn_b{i}", [128, 1024], BF16) for i in range(6)]
    gate_tm = sb3("gate_tm", [128, 128], BF16)
    ggf = sb3("ggf", [128, 128], BF16)
    grow_t = sb3("grow_t", [65, TB], BF16)
    grow = [grow_t[32 * i:32 * i + 1] for i in range(EG)]
    r_es = sb3("r_es", [128, 16], F32)
    r_pr = sb3("r_pr", [128, 16], F32)
    r_t1 = sb3("r_t1", [128, 16], F32)
    r_m = sb3("r_m", [128, 1], F32)
    r_s = sb3("r_s", [128, 1], F32)
    sgb_b = [sb3(f"sgb_b{i}", [128, 128], BF16) for i in range(2)]
    ot_b = sb3("ot_b", [128, HID], BF16)
    xb_b = sb3("xb_b", [128, HID], F32)
    of_b = [sb3(f"of_b{i}", [128, HID], F32) for i in range(2)]

    _barrier(dict(cnt))

    @blk.vector
    def _(v):
        wt(v, "V", "cV", cnt["cV"])
        inc(v.memset(ones1[:], 1.0), "cV")

    for tb in range(NTB):
        @blk.sync
        def _(y, tb=tb):
            wt(y, "Y", "cV", cnt["cV"])
            wt(y, "Y", "cS", cnt["cS"])
            ap = bass.AP(xns_dram.tensor, tb * TB,
                         [[KCN * TC, 128], [TC, KCN], [1, TB]])
            inc(y.dma_start(out=xn_tb[:], in_=ap), "dxt", 16)
        xnt_ld = cnt["dxt"]

        rt_mT0 = cnt["mT"]
        rt_cV0 = cnt["cV"]
        rt_cS0 = cnt["cS"]
        RV = 10

        @blk.tensor
        def _(t, tb=tb, rt_cV0=rt_cV0, rt_cS0=rt_cS0, xnt_ld=xnt_ld):
            wt(t, "T", "dxt", xnt_ld)
            for g in range(GTB):
                if g >= 2:
                    wt(t, "T", "cS", rt_cS0 + (g - 2) + 1)
                for kc in range(KCN):
                    ins = t.matmul(pb[4 + g % 2][:, 0:16],
                                   xn_tb[:, kc * TB + g * 128:
                                         kc * TB + (g + 1) * 128],
                                   wr_sb[:, kc * 16:(kc + 1) * 16],
                                   start=(kc == 0), stop=(kc == KCN - 1))
                inc(ins, "mT")
            wt(t, "T", "cV", rt_cV0 + GTB * RV)
            inc(t.transpose(pbh[2][0:GTB * 16, 0:128], gate_tm[:, 0:GTB * 16],
                            id_sb[:]), "mT")

        @blk.vector
        def _(v, tb=tb, rt_mT0=rt_mT0, rt_cS0=rt_cS0):
            for g in range(GTB):
                wt(v, "V", "mT", rt_mT0 + g + 1)
                wt(v, "V", "cV", cnt["cV"])
                inc(v.reduce_max(r_m[:], pb[4 + g % 2][:, 0:16], axis=AX.X,
                                 negate=True), "cV")
                wt(v, "V", "cS", rt_cS0 + g + 1)
                wt(v, "V", "cV", cnt["cV"])
                inc(v.reduce_sum(r_s[:], r_es[:], axis=AX.X), "cV")
                wt(v, "V", "cV", cnt["cV"])
                inc(v.reciprocal(r_s[:], r_s[:]), "cV")
                wt(v, "V", "cV", cnt["cV"])
                inc(v.tensor_scalar(r_pr[:], r_es[:], r_s[:], None, ALU.mult),
                    "cV")
                wt(v, "V", "cV", cnt["cV"])
                inc(v.reduce_max(r_m[:], r_pr[:], axis=AX.X), "cV")
                wt(v, "V", "cV", cnt["cV"])
                inc(v.tensor_single_scalar(r_t1[:], r_pr[:], r_m[:],
                                           ALU.is_equal), "cV")
                wt(v, "V", "cV", cnt["cV"])
                inc(v.scalar_tensor_tensor(r_t1[:], r_t1[:], -1e30, r_pr[:],
                                           ALU.mult, ALU.add), "cV")
                wt(v, "V", "cV", cnt["cV"])
                inc(v.reduce_max(r_m[:], r_t1[:], axis=AX.X), "cV")
                wt(v, "V", "cV", cnt["cV"])
                inc(v.tensor_single_scalar(r_t1[:], r_pr[:], r_m[:],
                                           ALU.is_ge), "cV")
                gdst = bass.AP(gate_tm, g, [[128, 128], [GTB, 16]])
                wt(v, "V", "cV", cnt["cV"])
                inc(v.tensor_mul(gdst, r_t1[:], r_pr[:]), "cV")
            wt(v, "V", "dgr", cnt["dgr"])
            wt(v, "V", "cV", cnt["cV"])
            inc(v.memset(ggf[:], 0.0), "cV")
            wt(v, "V", "mT", rt_mT0 + GTB + 1)
            wt(v, "V", "cV", cnt["cV"])
            inc(v.tensor_copy(ggf[0:GTB * 16, :], pbh[2][0:GTB * 16, 0:128]),
                "cV")

        @blk.scalar
        def _(s, tb=tb, rt_mT0=rt_mT0, rt_cV0=rt_cV0):
            for g in range(GTB):
                wt(s, "S", "mT", rt_mT0 + g + 1)
                wt(s, "S", "cV", rt_cV0 + g * RV + 1)
                if g >= 1:
                    wt(s, "S", "cV", rt_cV0 + (g - 1) * RV + 4)
                wt(s, "S", "cS", cnt["cS"])
                inc(s.activation(out=r_es[:], in_=pb[4 + g % 2][:, 0:16],
                                 func=AF.Exp, bias=r_m[:], scale=1.0), "cS")

        gate_cV = cnt["cV"]
        prev_gbc_mT = [None]

        for eg in range(NEG):
            up_mT0 = cnt["mT"]
            up_cS0 = cnt["cS"]

            @blk.sync
            def _(y, tb=tb, eg=eg, gate_cV=gate_cV):
                wt(y, "Y", "cV", gate_cV)
                if prev_gbc_mT[0] is not None:
                    wt(y, "Y", "mT", prev_gbc_mT[0])
                for e3 in range(EG):
                    e = eg * EG + e3
                    if e < NSH:
                        continue
                    er = e - NSH
                    src = ggf[er * GTB:(er + 1) * GTB, 0:128]
                    inc(y.dma_start(out=grow[e3][0:1, :], in_=src), "dgr", 16)
            grow_dS = cnt["dgr"]

            uld = []

            @blk.gpsimd
            def _(g, tb=tb, eg=eg, up_mT0=up_mT0):
                for i in range(EG * DCN):
                    e3, dc = divmod(i, DCN)
                    e = eg * EG + e3
                    if i >= 2:
                        wt(g, "G", "mT", up_mT0 + (i - 2) * NCH + NCH)
                    else:
                        wt(g, "G", "mT", up_mT0)
                    sname = f"du{i % 2}"
                    inc(g.dma_start(out=wup_b[i % 2][:], in_=wup_d[e, dc]),
                        sname, 16)
                    uld.append((sname, cnt[sname]))

            @blk.tensor
            def _(t, tb=tb, eg=eg, up_cS0=up_cS0, uld=uld):
                for i in range(EG * DCN):
                    wt(t, "T", *uld[i])
                    for n in range(NCH):
                        k = i * NCH + n
                        if k >= 2:
                            wt(t, "T", "cS", up_cS0 + k - 1)
                        else:
                            wt(t, "T", "cS", up_cS0)
                        for kc in range(KCN):
                            ins = t.matmul(
                                pb[k % 2][:, 0:CH],
                                wup_b[i % 2][:, kc * 128:(kc + 1) * 128],
                                xn_tb[:, kc * TB + n * CH:kc * TB + (n + 1) * CH],
                                start=(kc == 0), stop=(kc == KCN - 1))
                        inc(ins, "mT")

            @blk.scalar
            def _(s, tb=tb, eg=eg, up_mT0=up_mT0):
                for i in range(EG * DCN):
                    e3, dc = divmod(i, DCN)
                    for n in range(NCH):
                        k = i * NCH + n
                        wt(s, "S", "mT", up_mT0 + k + 1)
                        wt(s, "S", "cS", cnt["cS"])
                        dst = mids[:, e3 * DCN * TB + dc * TB + n * CH:
                                   e3 * DCN * TB + dc * TB + (n + 1) * CH]
                        inc(s.activation(out=dst, in_=pb[k % 2][:, 0:CH],
                                         func=AF.Gelu), "cS")

            gel_cS = cnt["cS"]
            gm_mT0 = cnt["mT"]
            gm_cV0 = cnt["cV"]

            @blk.tensor
            def _(t, tb=tb, eg=eg, grow_dS=grow_dS, gm_cV0=gm_cV0):
                wt(t, "T", "dgr", grow_dS)
                j = 0
                for e3 in range(EG):
                    e = eg * EG + e3
                    if e < NSH:
                        continue
                    for n in range(NCH):
                        if j >= 2:
                            wt(t, "T", "cV", gm_cV0 + j - 1)
                        inc(t.matmul(pb[2 + j % 2][:, 0:CH],
                                     ones1[32 * e3:32 * e3 + 1],
                                     grow[e3][0:1, n * CH:(n + 1) * CH],
                                     start=True, stop=True), "mT")
                        j += 1
            prev_gbc_mT[0] = cnt["mT"]

            @blk.vector
            def _(v, tb=tb, eg=eg, gm_mT0=gm_mT0, gel_cS=gel_cS):
                j = 0
                for e3 in range(EG):
                    e = eg * EG + e3
                    if e < NSH:
                        continue
                    for n in range(NCH):
                        wt(v, "V", "mT", gm_mT0 + j + 1)
                        wt(v, "V", "cS", gel_cS)
                        src = bass.AP(mids, e3 * DCN * TB + n * CH,
                                      [[EG * DCN * TB, 128], [TB, DCN],
                                       [1, CH]])
                        gb = bass.AP(pb[2 + j % 2], 0,
                                     [[512, 128], [0, DCN], [1, CH]])
                        wt(v, "V", "cV", cnt["cV"])
                        inc(v.tensor_tensor(src, src, gb, ALU.mult), "cV")
                        j += 1

            mid_cV = cnt["cV"]
            dn_mT0 = cnt["mT"]
            dn_cV0 = cnt["cV"]
            dld = []

            @blk.gpsimd
            def _(g, tb=tb, eg=eg, dn_mT0=dn_mT0):
                for i in range(MCN * EG):
                    mc, e3 = divmod(i, EG)
                    e = eg * EG + e3
                    if i >= 6:
                        wt(g, "G", "mT", dn_mT0 + ((i - 6) // EG) * NCH + NCH)
                    else:
                        wt(g, "G", "mT", dn_mT0)
                    sname = f"dd{i % 6}"
                    inc(g.dma_start(out=wdn_b[i % 6][:], in_=wdn_d[e, mc]),
                        sname, 16)
                    dld.append((sname, cnt[sname]))

            @blk.tensor
            def _(t, tb=tb, eg=eg, dn_cV0=dn_cV0, mid_cV=mid_cV,
                  gel_cS=gel_cS, dld=dld):
                wt(t, "T", "cV", mid_cV)
                wt(t, "T", "cS", gel_cS)
                for mc in range(MCN):
                    for e3 in range(EG):
                        wt(t, "T", *dld[mc * EG + e3])
                    for n in range(NCH):
                        idx = mc * NCH + n
                        if idx >= 2:
                            wt(t, "T", "cV", dn_cV0 + idx - 1)
                        else:
                            wt(t, "T", "cV", dn_cV0)
                        for e3 in range(EG):
                            for dc in range(DCN):
                                ins = t.matmul(
                                    pb[4 + idx % 2][:, 0:CH],
                                    wdn_b[(mc * EG + e3) % 6][:,
                                          dc * 128:(dc + 1) * 128],
                                    mids[:, e3 * DCN * TB + dc * TB + n * CH:
                                         e3 * DCN * TB + dc * TB + (n + 1) * CH],
                                    start=(e3 == 0 and dc == 0),
                                    stop=(e3 == EG - 1 and dc == DCN - 1))
                        inc(ins, "mT")

            @blk.vector
            def _(v, tb=tb, eg=eg, dn_mT0=dn_mT0):
                for mc in range(MCN):
                    for n in range(NCH):
                        idx = mc * NCH + n
                        wt(v, "V", "mT", dn_mT0 + idx + 1)
                        dst = acc_tb[:, mc * TB + n * CH:mc * TB + (n + 1) * CH]
                        wt(v, "V", "cV", cnt["cV"])
                        if eg == 0:
                            inc(v.tensor_copy(dst, pb[4 + idx % 2][:, 0:CH]),
                                "cV")
                        else:
                            inc(v.tensor_add(dst, dst,
                                             pb[4 + idx % 2][:, 0:CH]), "cV")

        ffn_cV = cnt["cV"]

        @blk.scalar
        def _(s, tb=tb, ffn_cV=ffn_cV):
            wt(s, "S", "cV", ffn_cV)
            for mc in range(MCN):
                wt(s, "S", "cS", cnt["cS"])
                inc(s.activation(out=xn_tb[:, mc * TB:(mc + 1) * TB],
                                 in_=acc_tb[:, mc * TB:(mc + 1) * TB],
                                 func=AF.Silu), "cS")

        silu_cS = cnt["cS"]

        @blk.sync
        def _(y, tb=tb, _sp=cnt["dsp"]):
            wt(y, "Y", "dsp", _sp)
            wt(y, "Y", "mT", cnt["mT"])
            ap = bass.AP(s_dram.tensor, tb * TB,
                         [[MCN * TC, 128], [TC, MCN], [1, TB]])
            inc(y.dma_start(out=mids[:, 0:MCN * TB], in_=ap), "dst", 16)
        stb_ld = cnt["dst"]

        cb_cV0 = cnt["cV"]
        cb_mT0 = cnt["mT"]
        cb_d = {0: cnt["dou"], 1: cnt["dou1"]}
        CBV = MCN * 2 + 1
        xbl = []
        oul = []

        @blk.sync
        def _(y, tb=tb, cb_cV0=cb_cV0):
            for u in range(GTB):
                if u >= 1:
                    wt(y, "Y", "cV", cb_cV0 + (u - 1) * CBV + CBV)
                else:
                    wt(y, "Y", "cV", cb_cV0)
                inc(y.dma_start(out=xb_b[:],
                                in_=xs_d[tb * TB + u * 128:
                                         tb * TB + (u + 1) * 128, :]),
                    "dxb", 16)
                xbl.append(cnt["dxb"])
                if u >= 1:
                    p = u - 1
                    wt(y, "Y", "cV", cb_cV0 + p * CBV + CBV)
                    inc(y.dma_start(out=out_d[tb * TB + p * 128:
                                              tb * TB + (p + 1) * 128, :],
                                    in_=of_b[p % 2][:]),
                        "dou" if p % 2 == 0 else "dou1", 16)
                    oul.append(cnt["dou" if p % 2 == 0 else "dou1"])
            p = GTB - 1
            wt(y, "Y", "cV", cb_cV0 + p * CBV + CBV)
            inc(y.dma_start(out=out_d[tb * TB + p * 128:
                                      tb * TB + (p + 1) * 128, :],
                            in_=of_b[p % 2][:]),
                "dou" if p % 2 == 0 else "dou1", 16)
            oul.append(cnt["dou" if p % 2 == 0 else "dou1"])

        @blk.vector
        def _(v, tb=tb, cb_mT0=cb_mT0, silu_cS=silu_cS, stb_ld=stb_ld,
              xbl=xbl, oul=oul):
            wt(v, "V", "cS", silu_cS)
            wt(v, "V", "dst", stb_ld)
            for u in range(GTB):
                for mc in range(MCN):
                    j = u * MCN + mc - 2
                    if j >= 0:
                        wt(v, "V", "mT", cb_mT0 + j + 1)
                    wt(v, "V", "cV", cnt["cV"])
                    inc(v.tensor_mul(sgb_b[mc % 2][:],
                                     mids[:, mc * TB + u * 128:
                                          mc * TB + u * 128 + 128],
                                     xn_tb[:, mc * TB + u * 128:
                                           mc * TB + u * 128 + 128]), "cV")
                    wt(v, "V", "mT", cb_mT0 + u * MCN + mc + 1)
                    wt(v, "V", "cV", cnt["cV"])
                    inc(v.tensor_copy(ot_b[:, mc * 128:(mc + 1) * 128],
                                      pbh[6 + mc % 2][:, 0:128]), "cV")
                wt(v, "V", "dxb", xbl[u])
                q = "dou" if u % 2 == 0 else "dou1"
                if u >= 2:
                    wt(v, "V", q, oul[u - 2])
                else:
                    wt(v, "V", q, cb_d[u % 2])
                wt(v, "V", "cV", cnt["cV"])
                inc(v.tensor_add(of_b[u % 2][:], xb_b[:], ot_b[:]), "cV")

        @blk.tensor
        def _(t, tb=tb, cb_cV0=cb_cV0):
            for u in range(GTB):
                for mc in range(MCN):
                    wt(t, "T", "cV", cb_cV0 + u * CBV + 2 * mc + 1)
                    j = u * MCN + mc - 2
                    if j >= 0:
                        uj, mj = divmod(j, MCN)
                        wt(t, "T", "cV", cb_cV0 + uj * CBV + 2 * mj + 2)
                    inc(t.transpose(pbh[6 + mc % 2][:, 0:128],
                                    sgb_b[mc % 2][:], id_sb[:]), "mT")

    fin_dou = cnt["dou"]
    fin_dou1 = cnt["dou1"]

    @blk.sync
    def _(y):
        y.wait_ge(sems["dou"], fin_dou)
        y.wait_ge(sems["dou1"], fin_dou1)

    nc._kernel_ctx = ctx
    return nc


def _prep_tables(TC, core_id):
    t0 = core_id * TC
    idx = np.arange(t0, t0 + TC)
    pos = (idx % 8192).astype(np.float32)
    expo = (np.arange(0, 128, 2).astype(np.float32) / np.float32(128.0))
    inv_freq = (np.float32(1.0)
                / np.power(np.float32(10000.0), expo)).astype(np.float32)
    freqs = (pos[:, None] * inv_freq[None, :]).astype(np.float32)
    cos64 = np.cos(freqs).T.astype(np.float32)
    sin64 = np.sin(freqs).T.astype(np.float32)
    cosf = np.concatenate([cos64, cos64], axis=0)
    sinf = np.concatenate([sin64, sin64], axis=0)
    return np.ascontiguousarray(cosf), np.ascontiguousarray(sinf)


def _mask16():
    m = np.zeros((128, 256), dtype=np.float32)
    for t in range(16):
        for hk in range(8):
            for h in range(16):
                m[t * 8 + hk, h * 16 + t] = MASKV
    return m.astype(ml_dtypes.bfloat16)


def _pmat():
    p = np.zeros((128, 128), dtype=np.float32)
    for dp in range(64):
        p[dp + 64, dp] = -1.0
    for dp in range(64, 128):
        p[dp - 64, dp] = 1.0
    return p.astype(ml_dtypes.bfloat16)


def scram_rows(core, TC, quad):
    b, qc = core // quad, core % quad
    RPH = quad * TC // 16
    R = TC // 16
    h = np.arange(16)[:, None, None]
    r = np.arange(R)[None, None, :]
    idx = b * quad * TC + h * RPH + qc * R + r
    return idx.reshape(-1)


def host_prepare(inputs, TC=2048, ncores=NCORE, quad=None):
    if quad is None:
        quad = min(4, ncores)
    Wq = np.asarray(inputs["L0_attn_Wq"])
    Wk = np.asarray(inputs["L0_attn_Wk"])
    Wv = np.asarray(inputs["L0_attn_Wv"])
    Wo = np.asarray(inputs["L0_attn_Wo"])
    R = np.asarray(inputs["L0_attn_rand_matrix"])
    Wr = np.asarray(inputs["L0_ffn_W_router"])
    x = np.asarray(inputs["L0_attn_hidden_states"]).reshape(-1, HID)

    wq = np.ascontiguousarray(Wq.reshape(16, 128, 16, 128).transpose(0, 3, 2, 1)
                              .reshape(16, 128, 2048))
    wk = np.ascontiguousarray(Wk.reshape(8, 128, 16, 128).transpose(0, 3, 2, 1)
                              .reshape(8, 128, 2048))
    wv = np.ascontiguousarray(Wv.reshape(8, 128, 16, 128).transpose(0, 3, 2, 1)
                              .reshape(8, 128, 2048))
    wo = np.ascontiguousarray(Wo.reshape(16, 128, 16, 128).transpose(0, 3, 2, 1)
                              .reshape(16, 128, 2048))
    wr = np.ascontiguousarray(Wr.reshape(16, 128, 16).transpose(1, 0, 2)
                              .reshape(128, 256))
    wup_all = np.concatenate([np.asarray(inputs["L0_ffn_Wsh_up"]),
                              np.asarray(inputs["L0_ffn_Wrt_up"])], axis=0)
    wdn_all = np.concatenate([np.asarray(inputs["L0_ffn_Wsh_down"]),
                              np.asarray(inputs["L0_ffn_Wrt_down"])], axis=0)
    wup = np.ascontiguousarray(wup_all.reshape(NE, 16, 128, 8, 128)
                               .transpose(0, 3, 2, 1, 4).reshape(NE, 8, 128, 2048))
    wdn = np.ascontiguousarray(wdn_all.reshape(NE, 8, 128, 16, 128)
                               .transpose(0, 3, 2, 1, 4).reshape(NE, 16, 128, 1024))
    rt = np.ascontiguousarray(R.T)
    bq = np.ascontiguousarray(np.asarray(inputs["L0_attn_bq"]).reshape(16, 128).T)
    bk = np.ascontiguousarray(np.asarray(inputs["L0_attn_bk"]).reshape(8, 128).T)
    bv = np.ascontiguousarray(np.asarray(inputs["L0_attn_bv"]).reshape(8, 128).T)
    bo = np.ascontiguousarray(np.asarray(inputs["L0_attn_bo"]).reshape(16, 128).T)
    maskc = _mask16()
    identc = np.eye(128, dtype=np.float32).astype(ml_dtypes.bfloat16)
    pmat = _pmat()
    offr = np.zeros((128, 1), dtype=np.float32)
    offr[64:, 0] = 0.25
    offs = np.zeros((128, 1), dtype=np.float32)
    offs[64:, 0] = math.pi / 2.0

    in_maps = []
    for c in range(ncores):
        cosf, sinf = _prep_tables(TC, c)
        in_maps.append(dict(
            x=np.ascontiguousarray(x[c * TC:(c + 1) * TC]),
            xs=np.ascontiguousarray(x[scram_rows(c, TC, quad)]),
            wq=wq, wk=wk, wv=wv, wo=wo, wr=wr, wup=wup, wdn=wdn, rt=rt,
            bq=bq, bk=bk, bv=bv, bo=bo, cosf=cosf, sinf=sinf,
            maskc=maskc, identc=identc, pmat=pmat, offr=offr, offs=offs,
        ))
    return in_maps


def kernel(**inputs):
    TC = 16384 // NCORE
    quad = 4
    in_maps = host_prepare(inputs, TC=TC, quad=quad)
    nc = build_program(TC=TC, quad=quad)
    res = run_bass_kernel_spmd(nc, in_maps, core_ids=list(range(NCORE)))
    out = np.empty((16384, HID), dtype=np.float32)
    for c in range(NCORE):
        out[scram_rows(c, TC, quad)] = res.results[c]["out"]
    return out.reshape(2, 8192, HID)
